# version 11 (final)
# speedup vs baseline: 1.8702x; 1.0531x over previous
import numpy as np
import ml_dtypes
import concourse.bass as bass
import concourse.mybir as mybir
from concourse.bass_utils import run_bass_kernel_spmd

F32 = mybir.dt.float32
BF16 = mybir.dt.bfloat16
AF = mybir.ActivationFunctionType
ALU = mybir.AluOpType
AX = mybir.AxisListType

NCORES = 8
DM = 1024
SEQ = 2048
DI = 2048
NH = 32
HD = 64
NST = 128
DCONV = 4
CONV_DIM = DI + 2 * NST
PROJ = 2 * DI + 2 * NST + NH
Q = 128
NCH = SEQ // Q
AH = 16
AHD = 64
DH = 4096

SCK = 512
CPS = SCK // Q

ENGINES = ("tensor", "vector", "scalar", "gpsimd", "sync")


class Tok(tuple):
    __slots__ = ()


class Prog:

    def __init__(self, nc):
        self.nc = nc
        self.items = {e: [] for e in ENGINES}
        self.cnt = {}
        self.esem = {e: self._sem("c_" + e) for e in ENGINES}
        self.dsem = {e: self._sem("d_" + e) for e in ("sync", "scalar", "gpsimd", "vector", "tensor")}
        self.nlanes = 6
        self.lanes = {e: [self._sem(f"d_{e}{i}") for i in range(self.nlanes)]
                      for e in ("sync", "scalar", "gpsimd")}
        self.lane_idx = {e: 0 for e in ("sync", "scalar", "gpsimd")}
        self.roundup = set()
        self.chans = {}

    def _sem(self, name):
        h = self.nc.alloc_semaphore(name=name)
        self.cnt[h] = 0
        return h

    def op(self, eng, fn, deps=(), sync_self=True):
        sem = self.esem[eng]
        waits = self._mkwaits(eng, deps, sync_self)
        self.cnt[sem] += 1
        tok = Tok((sem, self.cnt[sem]))
        self.items[eng].append((waits, fn, sem, 1))
        return tok

    def dma(self, eng, fn, deps=(), sync_self=True, ch=None):
        if ch is None:
            i = self.lane_idx[eng]
            self.lane_idx[eng] = (i + 1) % self.nlanes
            sem = self.lanes[eng][i]
        else:
            sem = self.chans.get(ch)
            if sem is None:
                sem = self._sem("ch_" + ch)
                self.chans[ch] = sem
        deps = tuple(deps) + (Tok((sem, self.cnt[sem])),)
        waits = self._mkwaits(eng, deps, sync_self)
        self.cnt[sem] += 16
        tok = Tok((sem, self.cnt[sem]))
        self.items[eng].append((waits, fn, sem, 16))
        return tok

    def wait(self, eng, deps):
        waits = self._mkwaits(eng, deps, True)
        if waits:
            self.items[eng].append((waits, None, None, 0))

    def _mkwaits(self, eng, deps, sync_self):
        waits = {}
        own = self.esem[eng]
        for d in deps:
            if d is None:
                continue
            sem, val = d
            if sem is own and not sync_self:
                continue
            if sem in self.roundup:
                val = self.cnt[sem]
            waits[sem] = max(waits.get(sem, 0), val)
        return list(waits.items())

    def barrier(self):
        toks = []
        for e in ENGINES:
            sem = self.esem[e]
            toks.append(Tok((sem, self.cnt[sem])))
        for e, sem in self.dsem.items():
            toks.append(Tok((sem, self.cnt[sem])))
        for e in ENGINES:
            self.wait(e, toks)
        return toks

    def full_barrier(self):
        finals = [Tok((s, self.cnt[s])) for s in self.cnt]
        for e in ENGINES:
            self.wait(e, finals)

    def finalize(self):
        finals = [Tok((s, self.cnt[s])) for s in self.cnt]
        self.wait("sync", finals)
        nc = self.nc
        with nc.Block() as blk:
            for eng in ENGINES:
                items = self.items[eng]

                def body(e, items=items, eng=eng):
                    waited = {}
                    for waits, fn, sem, amt in items:
                        for (s, v) in waits:
                            if waited.get(s, 0) >= v:
                                continue
                            e.wait_ge(s, v)
                            waited[s] = v
                        if fn is not None:
                            inst = fn(e)
                            if sem is not None:
                                inst.then_inc(sem, amt)

                getattr(blk, eng)(body)


class Arena:

    def __init__(self, nc, name, cols_f32):
        self.t = nc.alloc_sbuf_tensor(name, [128, cols_f32], F32)
        self.off = 0
        self.marks = {}

    def f32(self, cols):
        a = self.t[:, self.off:self.off + cols]
        self.off += cols
        return a

    def bf16(self, cols):
        assert cols % 2 == 0
        a = self.t[:, self.off:self.off + cols // 2].bitcast(BF16)
        self.off += cols // 2
        return a

    def mark(self, name):
        self.marks[name] = self.off

    def reset(self, name):
        self.off = self.marks[name]


class TBuf:

    __slots__ = ("w", "r")

    def __init__(self):
        self.w = None
        self.r = []

    def rd(self):
        return (self.w,) if self.w is not None else ()

    def wr(self):
        d = list(self.r)
        if self.w is not None:
            d.append(self.w)
        return tuple(d)

    def did_read(self, tok):
        self.r.append(tok)

    def did_write(self, tok):
        self.w = tok
        self.r = []



def _coltile(v):
    n = v.shape[0] // 128
    return np.ascontiguousarray(v.reshape(n, 128).T)


def _ktiles(w):
    K, N = w.shape
    return np.ascontiguousarray(w.reshape(K // 128, 128, N))


def _jblocks(wT, jt=128):
    K, N = wT.shape
    a = wT.reshape(K // 128, 128, N // jt, jt)
    return np.ascontiguousarray(a.transpose(2, 1, 0, 3))


def stage_inputs(inp):
    g = {}
    for d in ("f", "b"):
        Wi = inp[f"L0_attn_Wi_{d}"]
        for grp in range(2):
            z = Wi[:, grp * 1024:(grp + 1) * 1024]
            xs = Wi[:, DI + grp * 1024:DI + (grp + 1) * 1024]
            BC = Wi[:, DI + DI:DI + DI + 256]
            dt = Wi[:, DI + CONV_DIM + grp * 16:DI + CONV_DIM + (grp + 1) * 16]
            cat = np.concatenate([z, xs, BC, dt], axis=1)
            g[f"wi_{d}{grp}"] = np.ascontiguousarray(
                _ktiles(cat).transpose(1, 0, 2))
        Wo = inp[f"L0_attn_Wo_{d}"]
        g[f"wo_{d}0"] = np.ascontiguousarray(
            _ktiles(Wo[:1024]).transpose(1, 0, 2))
        g[f"wo_{d}1"] = np.ascontiguousarray(
            _ktiles(Wo[1024:]).transpose(1, 0, 2))
        cw = inp[f"L0_attn_conv_{d}"]
        g[f"convw_{d}"] = np.ascontiguousarray(
            cw.reshape(DCONV, 18, 128).transpose(2, 0, 1))
        g[f"convb_{d}"] = _coltile(inp[f"L0_attn_convb_{d}"])
        g[f"alog_{d}"] = np.ascontiguousarray(
            inp[f"L0_attn_Alog_{d}"].reshape(2, 16).T)
        g[f"dvec_{d}"] = inp[f"L0_attn_D_{d}"].reshape(1, 32)
        g[f"dtb_{d}"] = np.ascontiguousarray(
            inp[f"L0_attn_dtb_{d}"].reshape(2, 16).T)
    g["wm"] = np.ascontiguousarray(
        _ktiles(inp["L0_attn_Wm"]).transpose(1, 0, 2))
    g["bm"] = _coltile(inp["L0_attn_bm"])
    g["w12"] = np.stack([_jblocks(inp["L0_ffn_w1_w"].T),
                         _jblocks(inp["L0_ffn_w2_w"].T)],
                        axis=2).astype(ml_dtypes.bfloat16)
    g["h1"] = _jblocks(inp["L0_ffn_h1_w"].T).astype(ml_dtypes.bfloat16)
    g["h2"] = _jblocks(inp["L0_ffn_h2_w"].T).astype(ml_dtypes.bfloat16)
    g["w3"] = _jblocks(inp["L0_ffn_w3_w"].T).astype(ml_dtypes.bfloat16)
    g["w1s"] = _coltile(inp["L0_ffn_w1_s"])
    g["w2s"] = _coltile(inp["L0_ffn_w2_s"])
    g["h1s"] = _coltile(inp["L0_ffn_h1_s"])
    g["h2s"] = _coltile(inp["L0_ffn_h2_s"])
    g["w3s"] = _coltile(inp["L0_ffn_w3_s"])
    Wqkv = inp["L1_attn_Wqkv"]
    g["wqkv"] = np.ascontiguousarray(
        Wqkv.reshape(8, 128, 8, 384).transpose(2, 1, 0, 3))
    g["bqkv"] = np.ascontiguousarray(
        inp["L1_attn_bqkv"].reshape(8, 3, 128).transpose(2, 0, 1))
    g["wout"] = _ktiles(inp["L1_attn_Wout"])
    g["bout"] = _coltile(inp["L1_attn_bout"])
    g["wf2"] = _ktiles(inp["L1_ffn_weights"].T)
    g["wf2s"] = _coltile(inp["L1_ffn_scale"])
    g["wf2b"] = _coltile(inp["L1_ffn_bias"])
    g["ident"] = np.eye(128, dtype=np.float32)
    sel0 = np.zeros((128, 128), np.float32)
    sel0[0, :] = 1.0
    g["sel0"] = sel0
    sel64 = np.zeros((128, 128), np.float32)
    sel64[64, :] = 1.0
    g["sel64"] = sel64
    i = np.arange(Q)
    g["maskf"] = np.where(i[:, None] <= i[None, :], 1.0, 0.0).astype(np.float32)
    g["maskb"] = np.where(i[:, None] >= i[None, :], 1.0, 0.0).astype(np.float32)
    g["negind"] = np.ascontiguousarray(
        -np.kron(np.eye(16, dtype=np.float32), np.ones((1, 128), np.float32)))
    for k in g:
        if g[k].dtype == ml_dtypes.bfloat16:
            g[k] = np.ascontiguousarray(g[k])
        else:
            g[k] = np.ascontiguousarray(g[k], np.float32)
    return g



class Ctx:
    def __init__(self, nc):
        self.nc = nc
        self.p = Prog(nc)
        self.bufs = {}
        self._fc = {}
        self._fct = None
        self.sba = lambda n, sh, dt: nc.alloc_sbuf_tensor(n, sh, dt)

    def tb(self, name):
        return self.bufs.setdefault(name, TBuf())

    def cf(self, val):
        if self._fct is None:
            self._fct = self.nc.alloc_sbuf_tensor("fconsts", [128, 16], F32)
        if val not in self._fc:
            i = len(self._fc)
            ap = self._fct[:, i:i + 1]
            tok = self.p.op("vector",
                            lambda e, ap=ap, val=val: e.memset(ap, val))
            self._fc[val] = (i, tok)
        i, tok = self._fc[val]
        return self._fct[:, i:i + 1], tok


def alloc_consts(ctx):
    nc = ctx.nc
    ctx.cf(0.0)
    ctx.identf = nc.alloc_sbuf_tensor("identf", [128, 128], F32)
    ctx.identb = nc.alloc_sbuf_tensor("identb", [128, 128], BF16)
    ctx.sel0 = nc.alloc_sbuf_tensor("sel0", [128, 128], BF16)
    ctx.ones1 = nc.alloc_sbuf_tensor("ones1", [128, 1], BF16)
    ctx.c_ident = nc.dram_tensor("c_ident", [128, 128], F32, kind="ExternalInput").ap()
    ctx.p_tr = nc.alloc_psum_tensor("p_tr", [128, 1024], F32)
    ctx.pmA = [nc.alloc_psum_tensor(f"pmA{i}", [128, 512], F32) for i in range(4)]
    ctx.pmB = nc.alloc_psum_tensor("pmB", [128, 512], F32)
    ctx.xtm = nc.alloc_sbuf_tensor("xtm", [128, DM], F32)
    ctx.sel64 = nc.alloc_sbuf_tensor("sel64", [128, 128], F32)
    ctx.c_sel64 = nc.dram_tensor("c_sel64", [128, 128], F32, kind="ExternalInput").ap()
    ctx.c_sel0 = nc.dram_tensor("c_sel0", [128, 128], F32, kind="ExternalInput").ap()


def load_consts(ctx):
    p = ctx.p
    t1 = p.dma("sync", lambda e: e.dma_start(out=ctx.identf[:], in_=ctx.c_ident))
    t2 = p.dma("gpsimd", lambda e: e.dma_start(out=ctx.identb[:], in_=ctx.c_ident))
    t3 = p.dma("gpsimd", lambda e: e.dma_start(out=ctx.sel0[:], in_=ctx.c_sel0))
    t4 = p.op("vector", lambda e: e.memset(ctx.ones1[:], 1.0))
    t5 = p.dma("sync", lambda e: e.dma_start(out=ctx.sel64[:], in_=ctx.c_sel64))
    ctx.consts_tok = (t1, t2, t3, t4, t5)


def alloc_xres(ctx, with_bf16=True):
    nc = ctx.nc
    ctx.x_res = nc.alloc_sbuf_tensor("x_res", [128, 8 * SEQ], F32)
    ctx.with_bf16 = with_bf16
    if with_bf16:
        ctx.x_resb = nc.alloc_sbuf_tensor("x_resb", [128, 8 * SEQ], BF16)


def xres_f(ctx, j, c0=0, c1=SEQ):
    return ctx.x_res[:, j * SEQ + c0:j * SEQ + c1]


def xres_b(ctx, j, c0=0, c1=SEQ):
    return ctx.x_resb[:, j * SEQ + c0:j * SEQ + c1]


def load_x(ctx):
    nc, p = ctx.nc, ctx.p
    ctx.x_in = nc.dram_tensor("x_in", [SEQ, DM], F32, kind="ExternalInput").ap()
    make_bf16 = ctx.with_bf16
    xtm = ctx.xtm
    pb = ctx.p_tr
    tb_xtm = [ctx.tb("xtm0"), ctx.tb("xtm1")]
    tb_pb = [ctx.tb(f"p_tr{i}") for i in range(2)]
    xres_tb = ctx.tb("x_res")
    xresb_tb = ctx.tb("x_resb")
    cdep = ctx.consts_tok
    for tc in range(NCH):
        sl = 0
        dst = xtm[:, sl * DM:(sl + 1) * DM]
        t_ld = p.dma("sync", lambda e, dst=dst, tc=tc: e.dma_start(
            out=dst, in_=ctx.x_in[tc * 128:(tc + 1) * 128, :]),
            deps=tb_xtm[sl].wr())
        tb_xtm[sl].did_write(t_ld)
        for j in range(8):
            pslot = (tc * 8 + j) % 2
            pv = pb[:, pslot * 512:pslot * 512 + 128]
            t_tr = p.op("tensor", lambda e, dst=dst, j=j, pv=pv: e.transpose(
                pv, dst[:, j * 128:(j + 1) * 128], ctx.identf[:]),
                deps=(t_ld,) + tb_pb[pslot].wr() + cdep)
            tb_pb[pslot].did_write(t_tr)
            tb_xtm[sl].did_read(t_tr)
            dstf = xres_f(ctx, j, tc * 128, (tc + 1) * 128)
            t_cp = p.op("vector", lambda e, pv=pv, dstf=dstf: e.tensor_copy(
                out=dstf, in_=pv), deps=(t_tr,))
            tb_pb[pslot].did_read(t_cp)
            xres_tb.did_write(t_cp)
            if make_bf16:
                dstb = xres_b(ctx, j, tc * 128, (tc + 1) * 128)
                t_cb = p.op("vector", lambda e, dstf=dstf, dstb=dstb:
                            e.tensor_copy(out=dstb, in_=dstf), deps=(t_cp,))
                xresb_tb.did_write(t_cb)


def store_x(ctx):
    nc, p = ctx.nc, ctx.p
    ctx.x_out = nc.dram_tensor("x_out", [SEQ, DM], F32, kind="ExternalOutput").ap()
    xtm = ctx.xtm
    pb = ctx.p_tr
    tb_xtm = [ctx.tb("xtm0"), ctx.tb("xtm1")]
    tb_pb = [ctx.tb(f"p_tr{i}") for i in range(2)]
    xres_tb = ctx.tb("x_res")
    for tc in range(NCH):
        sl = 0
        dst = xtm[:, sl * DM:(sl + 1) * DM]
        for j in range(8):
            pslot = (tc * 8 + j) % 2
            pv = pb[:, pslot * 512:pslot * 512 + 128]
            src = xres_f(ctx, j, tc * 128, (tc + 1) * 128)
            t_tr = p.op("tensor", lambda e, src=src, pv=pv: e.transpose(
                pv, src, ctx.identf[:]),
                deps=xres_tb.rd() + tb_pb[pslot].wr() + tb_xtm[sl].wr())
            tb_pb[pslot].did_write(t_tr)
            xres_tb.did_read(t_tr)
            t_cp = p.op("vector", lambda e, pv=pv, dst=dst, j=j: e.tensor_copy(
                out=dst[:, j * 128:(j + 1) * 128], in_=pv), deps=(t_tr,))
            tb_pb[pslot].did_read(t_cp)
            tb_xtm[sl].did_write(t_cp)
        t_st = p.dma("sync", lambda e, dst=dst, tc=tc: e.dma_start(
            out=ctx.x_out[tc * 128:(tc + 1) * 128, :], in_=dst),
            deps=(tb_xtm[sl].w,))
        tb_xtm[sl].did_read(t_st)



def alloc_ln(ctx):
    if getattr(ctx, "ln_fixed", False):
        if ctx.ln_need_memset:
            ctx.ln_need_memset = False
            t = ctx.p.op("vector", lambda e: e.memset(ctx.ln_rowsb[:], 0.0),
                         deps=ctx.tb("ln_rowsb").wr())
            ctx.tb("ln_rowsb").did_write(t)
        return
    if getattr(ctx, "ln_u", None) is not None:
        return
    nc = ctx.nc
    ctx.ln_u = nc.alloc_sbuf_tensor("ln_u", [128, 8 * 512], F32)
    ctx.ln_ub = nc.alloc_sbuf_tensor("ln_ub", [128, 8 * 512], BF16)
    ctx.ln_rows = nc.alloc_sbuf_tensor("ln_rows", [128, 3 * 512], F32)
    ctx.ln_rowsb = nc.alloc_sbuf_tensor("ln_rowsb", [128, 2 * 512], BF16)
    ctx.ln_bc = nc.alloc_sbuf_tensor("ln_bc", [128, 2 * 512], F32)
    if getattr(ctx, "p_stat", None) is None:
        ctx.p_stat = nc.alloc_psum_tensor("p_stat", [128, 512], F32)
    ctx.p_bc = ctx.p_tr
    t = ctx.p.op("vector", lambda e: e.memset(ctx.ln_rowsb[:], 0.0))
    ctx.tb("ln_rowsb").did_write(t)


def ln_u(ctx, j, c0=0, c1=512):
    return ctx.ln_u[:, j * 512 + c0:j * 512 + c1]


def ln_chunk(ctx, tc, u_toks):
    alloc_ln(ctx)
    p = ctx.p
    T = 512
    ub_tb = ctx.tb("ln_ub")
    st_tb = ctx.tb("p_stat")
    rows_tb = ctx.tb("ln_rows")
    bc_tb = ctx.tb("ln_bc")
    xres_tb, xresb_tb = ctx.tb("x_res"), ctx.tb("x_resb")
    tub = []
    for j in range(8):
        t1 = p.op("vector", lambda e, j=j: e.tensor_copy(
            out=ctx.ln_ub[:, j * T:(j + 1) * T], in_=ln_u(ctx, j)),
            deps=(u_toks[j],) + ub_tb.wr())
        ub_tb.did_write(t1)
        tub.append(t1)
    def mk_mm(rows_src, first, last, out_row):
        def f(e, rows_src=rows_src, first=first, last=last, out_row=out_row):
            return e.matmul(ctx.p_stat[out_row:out_row + 1, :],
                            ctx.ones1[:], rows_src,
                            start=first, stop=last)
        return f
    t_mm = None
    for j in range(8):
        t_mm = p.op("tensor", mk_mm(ctx.ln_ub[:, j * T:(j + 1) * T],
                                    j == 0, j == 7, 0),
                    deps=(tub[j],) + (st_tb.wr() if j == 0 else ()))
    t_summ = t_mm
    ub_tb.did_read(t_summ)
    tsq = []
    for j in range(8):
        t2 = p.op("vector", lambda e, j=j: e.tensor_tensor(
            out=ctx.ln_ub[:, j * T:(j + 1) * T],
            in0=ctx.ln_ub[:, j * T:(j + 1) * T],
            in1=ctx.ln_ub[:, j * T:(j + 1) * T], op=ALU.mult),
            deps=(t_summ,) + ((ub_tb.wr()) if j == 0 else ()))
        tsq.append(t2)
    ub_tb.did_write(tsq[-1])
    for j in range(8):
        t_mm = p.op("tensor", mk_mm(ctx.ln_ub[:, j * T:(j + 1) * T],
                                    j == 0, j == 7, 64),
                    deps=(tsq[j],))
    st_tb.did_write(t_mm)
    ub_tb.did_read(t_mm)
    R = ctx.ln_rows
    def row(i):
        return R[0:1, i * T:(i + 1) * T]
    t1 = p.op("vector", lambda e: e.tensor_scalar(
        out=row(0), in0=ctx.p_stat[0:1, :], scalar1=1.0 / DM, scalar2=None,
        op0=ALU.mult), deps=(t_mm,) + rows_tb.wr())
    t2 = p.op("vector", lambda e: e.tensor_scalar(
        out=row(1), in0=ctx.p_stat[64:65, :], scalar1=1.0 / DM, scalar2=None,
        op0=ALU.mult), deps=(t_mm,))
    st_tb.did_read(t1)
    st_tb.did_read(t2)
    t3 = p.op("scalar", lambda e: e.activation(
        out=row(2), in_=row(0), func=AF.Square), deps=(t1,))
    t4 = p.op("vector", lambda e: e.tensor_tensor(
        out=row(2), in0=row(1), in1=row(2), op=ALU.subtract),
        deps=(t2, t3), sync_self=True)
    epsc, eps_tok = ctx.cf(1e-5)
    nhc, nh_tok = ctx.cf(-0.5)
    t5 = p.op("scalar", lambda e: e.activation(
        out=row(2), in_=row(2), func=AF.Ln, bias=epsc[0:1, :]),
        deps=(t4, eps_tok), sync_self=True)
    t6 = p.op("scalar", lambda e: e.activation(
        out=row(2), in_=row(2), func=AF.Exp, scale=nhc[0:1, :]),
        deps=(t5, nh_tok), sync_self=True)
    t7 = p.op("vector", lambda e: e.tensor_tensor(
        out=row(1), in0=row(0), in1=row(2), op=ALU.mult),
        deps=(t6, t2), sync_self=True)
    rows_tb.did_write(t7)
    rb_tb = ctx.tb("ln_rowsb")
    tr1 = p.op("vector", lambda e: e.tensor_copy(
        out=ctx.ln_rowsb[0:1, 0:T], in_=row(2)), deps=(t6,) + rb_tb.wr(),
        sync_self=True)
    tr2 = p.op("vector", lambda e: e.tensor_copy(
        out=ctx.ln_rowsb[0:1, T:2 * T], in_=row(1)), deps=(t7,),
        sync_self=True)
    rb_tb.did_write(tr2)
    rows_tb.did_read(tr2)
    ptr_tb = [ctx.tb("p_tr0"), ctx.tb("p_tr1")]
    tb1 = p.op("tensor", lambda e: e.matmul(
        ctx.p_bc[:, 0:T], ctx.sel0[:], ctx.ln_rowsb[:, 0:T],
        start=True, stop=True), deps=(tr1,) + ptr_tb[0].wr())
    tb2 = p.op("tensor", lambda e: e.matmul(
        ctx.p_bc[:, T:2 * T], ctx.sel0[:], ctx.ln_rowsb[:, T:2 * T],
        start=True, stop=True), deps=(tr2,) + ptr_tb[1].wr())
    ptr_tb[0].did_write(tb1)
    ptr_tb[1].did_write(tb2)
    rb_tb.did_read(tb2)
    tc1 = p.op("vector", lambda e: e.tensor_copy(
        out=ctx.ln_bc[:, 0:T], in_=ctx.p_bc[:, 0:T]),
        deps=(tb1,) + bc_tb.wr())
    tc2 = p.op("vector", lambda e: e.tensor_copy(
        out=ctx.ln_bc[:, T:2 * T], in_=ctx.p_bc[:, T:2 * T]), deps=(tb2,))
    bc_tb.did_write(tc2)
    ptr_tb[0].did_read(tc1)
    ptr_tb[1].did_read(tc2)
    out_toks = []
    for j in range(8):
        dstf = xres_f(ctx, j, tc * T, tc * T + T)
        t8 = p.op("vector", lambda e, j=j, dstf=dstf: e.tensor_tensor(
            out=dstf, in0=ln_u(ctx, j), in1=ctx.ln_bc[:, 0:T], op=ALU.mult),
            deps=(tc1, u_toks[j]) + xres_tb.wr(), sync_self=True)
        t9 = p.op("vector", lambda e, j=j, dstf=dstf: e.tensor_tensor(
            out=dstf, in0=dstf, in1=ctx.ln_bc[:, T:2 * T], op=ALU.subtract),
            deps=(tc2, t8))
        xres_tb.did_write(t9)
        bc_tb.did_read(t9)
        if ctx.with_bf16:
            dstb = xres_b(ctx, j, tc * T, tc * T + T)
            t10 = p.op("vector", lambda e, dstb=dstb, dstf=dstf: e.tensor_copy(
                out=dstb, in_=dstf), deps=(t9,) + xresb_tb.wr())
            xresb_tb.did_write(t10)
            out_toks.append(t10)
        else:
            out_toks.append(t9)
    return out_toks



def build_l1ffn(ctx):
    nc, p = ctx.nc, ctx.p
    w_in = nc.dram_tensor("wf2", [8, 128, DM], F32, kind="ExternalInput").ap()
    s_in = nc.dram_tensor("wf2s", [128, 8], F32, kind="ExternalInput").ap()
    b_in = nc.dram_tensor("wf2b", [128, 8], F32, kind="ExternalInput").ap()
    ws = ctx.sba("f2_ws", [128, 8 * DM], BF16)
    sb = ctx.sba("f2_sb", [128, 16], F32)
    pm = [ctx.pmA[0], ctx.pmA[1]]
    ws_tb = ctx.tb("f2_ws")
    pm_tb = [ctx.tb(f"pmA{i}") for i in range(2)]
    u_tb = ctx.tb("ln_u")
    xres_tb, xresb_tb = ctx.tb("x_res"), ctx.tb("x_resb")

    t_lw = p.dma("gpsimd", lambda e: e.dma_start(
        out=ws[:].rearrange("p (k n) -> p k n", k=8),
        in_=w_in.rearrange("k p n -> p k n")))
    t_ls = p.dma("sync", lambda e: e.dma_start(out=sb[:, 0:8], in_=s_in))
    t_lb = p.dma("sync", lambda e: e.dma_start(out=sb[:, 8:16], in_=b_in))
    t_s1 = p.op("vector", lambda e: e.tensor_scalar(
        out=ws[:], in0=ws[:], scalar1=0.0, scalar2=None, op0=ALU.is_ge),
        deps=(t_lw,))
    t_s2 = p.op("vector", lambda e: e.tensor_scalar(
        out=ws[:], in0=ws[:], scalar1=2.0, scalar2=1.0, op0=ALU.mult,
        op1=ALU.subtract), deps=(t_s1,), sync_self=True)
    ws_tb.did_write(t_s2)

    for tc in range(4):
        T = 512
        cols = slice(tc * T, (tc + 1) * T)
        u_toks = []
        for j in range(8):
            sl = j % 2
            mm = None
            for k in range(8):
                def f(e, j=j, k=k, sl=sl, cols=cols):
                    return e.matmul(pm[sl][:, :],
                                    ws[:, k * DM + j * 128:k * DM + (j + 1) * 128],
                                    ctx.x_resb[:, k * SEQ:k * SEQ + SEQ][:, cols],
                                    start=(k == 0), stop=(k == 7))
                deps = (t_s2, ctx.tb("x_resb").w) if k == 0 else ()
                if k == 0:
                    deps = deps + pm_tb[sl].wr()
                mm = p.op("tensor", f, deps=deps)
            pm_tb[sl].did_write(mm)
            xresb_tb.did_read(mm)
            t1 = p.op("vector", lambda e, j=j, sl=sl: e.tensor_scalar(
                out=ln_u(ctx, j), in0=pm[sl][:, :], scalar1=sb[:, j:j + 1],
                scalar2=sb[:, 8 + j:9 + j], op0=ALU.mult, op1=ALU.add),
                deps=(mm, t_ls, t_lb) + u_tb.wr())
            t2 = p.op("vector", lambda e, j=j: e.tensor_scalar(
                out=ln_u(ctx, j), in0=ln_u(ctx, j), scalar1=100.0,
                scalar2=-100.0, op0=ALU.min, op1=ALU.max), deps=(t1,))
            t3 = p.op("vector", lambda e, j=j, cols=cols: e.tensor_tensor(
                out=ln_u(ctx, j), in0=ln_u(ctx, j),
                in1=xres_f(ctx, j)[:, cols], op=ALU.add),
                deps=(t2,) + xres_tb.rd())
            pm_tb[sl].did_read(t1)
            u_tb.did_write(t3)
            xres_tb.did_read(t3)
            u_toks.append(t3)
        ln_chunk(ctx, tc, u_toks)



def build_block_program(which):
    nc = bass.Bass("TRN2", target_bir_lowering=False, debug=False)
    ctx = Ctx(nc)
    alloc_consts(ctx)
    alloc_xres(ctx, with_bf16=(which not in ("l0ffn", "mamba")))
    load_consts(ctx)
    if which != "mamba":
        alloc_ln(ctx)
    load_x(ctx)
    if which == "noop":
        pass
    elif which == "l1ffn":
        build_l1ffn(ctx)
    elif which == "l0ffn":
        build_l0ffn(ctx)
    elif which == "l1attn":
        build_l1attn(ctx)
    elif which == "mamba":
        build_mamba(ctx)
    store_x(ctx)
    ctx.p.finalize()
    return nc


def run_block(which, x_full, staged, ncores=NCORES):
    nc = build_block_program(which)
    names = {t for t in _block_input_names(nc)}
    in_maps = []
    for c in range(ncores):
        m = {"x_in": np.ascontiguousarray(x_full[c]),
             "c_ident": staged["ident"], "c_sel0": staged["sel0"],
             "c_sel64": staged["sel64"]}
        for n in names:
            if n in m or n == "partition_id":
                continue
            m[n] = staged[_stage_key(n)]
        in_maps.append(m)
    res = run_bass_kernel_spmd(nc, in_maps, core_ids=list(range(ncores)))
    return np.stack([np.asarray(r["x_out"]) for r in res.results])


def _block_input_names(nc):
    out = []
    for f in nc.m.functions:
        for al in f.allocations:
            if not isinstance(al, mybir.MemoryLocationSet):
                continue
            if al.kind == "ExternalInput":
                out.append(al.memorylocations[0].name)
    return out


_STAGE_MAP = {"wf2": "wf2", "wf2s": "wf2s", "wf2b": "wf2b"}


def _stage_key(name):
    return _STAGE_MAP.get(name, name)



def build_l0ffn(ctx):
    import os
    nchunks = int(os.environ.get("F1CHUNKS", "4"))
    phases = os.environ.get("F1PHASES", "w12,h1,h2,w3,ln").split(",")
    nc, p = ctx.nc, ctx.p
    T = 512
    w12_in = nc.dram_tensor("w12", [32, 128, 2, 8, 128], BF16, kind="ExternalInput").ap()
    h1_in = nc.dram_tensor("h1", [32, 128, 32, 128], BF16, kind="ExternalInput").ap()
    h2_in = nc.dram_tensor("h2", [32, 128, 32, 128], BF16, kind="ExternalInput").ap()
    w3_in = nc.dram_tensor("w3", [8, 128, 32, 128], BF16, kind="ExternalInput").ap()
    sc_in = {}
    for nm, w in (("w1s", 32), ("w2s", 32), ("h1s", 32), ("h2s", 32), ("w3s", 8)):
        sc_in[nm] = nc.dram_tensor(nm, [128, w], F32, kind="ExternalInput").ap()
    scs = ctx.sba("f1_scs", [128, 32 * 4 + 8 + 32], F32)
    w12b = ctx.sba("f1_w12", [128, 2 * 2048], BF16)
    h1b = ctx.sba("f1_h1b", [128, 2 * 4096], BF16)
    hA = ctx.sba("f1_hA", [128, 32 * T], BF16)
    hB = ctx.sba("f1_hB", [128, 32 * T], BF16)
    xb = ctx.sba("f1_xb", [128, 8 * T], BF16)
    tsc = ctx.sba("f1_tsc", [128, 2 * T], F32)
    pm = [ctx.pmA[0], ctx.pmA[1]]

    sc_ts = []
    for i, (nm, c0, c1) in enumerate((("w1s", 0, 32), ("w2s", 32, 64),
                                      ("h1s", 64, 96), ("h2s", 96, 128),
                                      ("w3s", 128, 136))):
        sc_ts.append(p.dma("sync", lambda e, nm=nm, c0=c0, c1=c1: e.dma_start(
            out=scs[:, c0:c1], in_=sc_in[nm])))
    t_h = p.op("vector", lambda e: e.tensor_scalar(
        out=scs[:, 136:168], in0=scs[:, 0:32], scalar1=0.5, scalar2=None,
        op0=ALU.mult), deps=tuple(sc_ts))
    sc_tok = tuple(sc_ts) + (t_h,)

    def col(nm, j):
        base = {"w1s": 0, "w2s": 32, "h1s": 64, "h2s": 96, "w3s": 128,
                "hw1s": 136}[nm]
        return scs[:, base + j:base + j + 1]

    xres_tb = ctx.tb("x_res")
    xb_tb = ctx.tb("f1_xb")
    hA_tb, hB_tb = ctx.tb("f1_hA"), ctx.tb("f1_hB")
    w12_tb = [ctx.tb(f"f1w12_{s}") for s in range(2)]
    h1_tb = [ctx.tb(f"f1h1_{s}") for s in range(2)]
    tscA_tb, tscB_tb = ctx.tb("f1_tscA"), ctx.tb("f1_tscB")
    pm_tb = [ctx.tb(f"pmA{i}") for i in range(2)]
    ptr_tb = [ctx.tb(f"p_tr{i}") for i in range(2)]
    u_tb = ctx.tb("ln_u")
    ptr = [ctx.p_tr[:, 0:512], ctx.p_tr[:, 512:1024]]

    def pick(i):
        if i < 2:
            return pm[i][:, :], pm_tb[i]
        return ptr[i - 2], ptr_tb[i - 2]

    for tc in range(nchunks):
        cols = slice(tc * T, (tc + 1) * T)
        xb_toks = []
        for k in range(8):
            t = p.op("vector", lambda e, k=k, cols=cols: e.tensor_copy(
                out=xb[:, k * T:(k + 1) * T], in_=xres_f(ctx, k)[:, cols]),
                deps=xres_tb.rd() + (xb_tb.wr() if k == 0 else ()))
            xres_tb.did_read(t)
            xb_toks.append(t)
        xb_tb.did_write(xb_toks[-1])
        xb_dep = xb_toks[-1]

        for j in range(32 if "w12" in phases else 0):
            s = j % 2
            wv = w12b[:, s * 2048:(s + 1) * 2048].rearrange(
                "p (m k n) -> p m k n", m=2, k=8)
            t_lw = p.dma("gpsimd", lambda e, j=j, wv=wv: e.dma_start(
                out=wv, in_=w12_in[j]), deps=w12_tb[s].wr(), ch=f"w12{s}")
            w12_tb[s].did_write(t_lw)
            ba, tba = pick(j % 2)
            bb, tbb = pick(2 + (j % 2))
            mm_a = mm_b = None
            for k in range(8):
                mm_a = p.op("tensor", lambda e, wv=wv, k=k, ba=ba, cols=cols:
                            e.matmul(ba, wv[:, 0, k, :], xb[:, k * T:(k + 1) * T],
                                     start=(k == 0), stop=(k == 7)),
                            deps=((t_lw, xb_dep) + tba.wr()) if k == 0 else ())
            for k in range(8):
                mm_b = p.op("tensor", lambda e, wv=wv, k=k, bb=bb, cols=cols:
                            e.matmul(bb, wv[:, 1, k, :], xb[:, k * T:(k + 1) * T],
                                     start=(k == 0), stop=(k == 7)),
                            deps=((t_lw, xb_dep) + tbb.wr()) if k == 0 else ())
            tba.did_write(mm_a)
            tbb.did_write(mm_b)
            w12_tb[s].did_read(mm_b)
            xb_tb.did_read(mm_b)
            tt = p.op("scalar", lambda e, ba=ba, j=j: e.activation(
                out=tsc[:, 0:T], in_=ba, func=AF.Tanh, scale=col("hw1s", j)),
                deps=(mm_a,) + sc_tok + tscA_tb.wr())
            tscA_tb.did_write(tt)
            ta = p.op("vector", lambda e, ba=ba, j=j: e.tensor_scalar(
                out=tsc[:, T:2 * T], in0=ba, scalar1=col("w1s", j), scalar2=0.5,
                op0=ALU.mult, op1=ALU.mult), deps=(mm_a, tt) + tscB_tb.wr())
            tba.did_read(tt)
            tba.did_read(ta)
            tg = p.op("vector", lambda e: e.scalar_tensor_tensor(
                out=tsc[:, T:2 * T], in0=tsc[:, 0:T], scalar=1.0,
                in1=tsc[:, T:2 * T], op0=ALU.add, op1=ALU.mult),
                deps=(tt, ta))
            tscA_tb.did_read(tg)
            tscB_tb.did_write(tg)
            th = p.op("vector", lambda e, bb=bb, j=j: e.scalar_tensor_tensor(
                out=hA[:, j * T:(j + 1) * T], in0=bb, scalar=col("w2s", j),
                in1=tsc[:, T:2 * T], op0=ALU.mult, op1=ALU.mult),
                deps=(mm_b, tg) + (hA_tb.wr() if j == 0 else ()))
            tbb.did_read(th)
            tscB_tb.did_read(th)
            if j == 31:
                hA_tb.did_write(th)

        stages = []
        if "h1" in phases:
            stages.append((h1_in, hA, hA_tb, hB, hB_tb, "h1s"))
        if "h2" in phases:
            stages.append((h2_in, hB, hB_tb, hA, hA_tb, "h2s"))
        for stage, (win, src_t, src_tb, dst_t, dst_tb, scn) in enumerate(stages):
            for j in range(32):
                s = j % 2
                wv = h1b[:, s * 4096:(s + 1) * 4096].rearrange(
                    "p (k n) -> p k n", k=32)
                t_lw = p.dma("gpsimd", lambda e, j=j, wv=wv, win=win: e.dma_start(
                    out=wv, in_=win[j]), deps=h1_tb[s].wr(), ch=f"h1{s}")
                h1_tb[s].did_write(t_lw)
                ba, tba = pick(j % 4)
                mm = None
                for k in range(32):
                    mm = p.op("tensor", lambda e, wv=wv, k=k, ba=ba, src_t=src_t:
                              e.matmul(ba, wv[:, k, :], src_t[:, k * T:(k + 1) * T],
                                       start=(k == 0), stop=(k == 31)),
                              deps=((t_lw, src_tb.w) + tba.wr()) if k == 0 else ())
                tba.did_write(mm)
                h1_tb[s].did_read(mm)
                src_tb.did_read(mm)
                tg = p.op("scalar", lambda e, ba=ba, j=j, scn=scn, dst_t=dst_t:
                          e.activation(out=dst_t[:, j * T:(j + 1) * T], in_=ba,
                                       func=AF.Gelu, scale=col(scn, j)),
                          deps=(mm,) + (dst_tb.wr() if j == 0 else ()))
                tba.did_read(tg)
                if j == 31:
                    dst_tb.did_write(tg)

        u_toks = []
        for j in range(8 if "w3" in phases else 0):
            s = j % 2
            wv = h1b[:, s * 4096:(s + 1) * 4096].rearrange(
                "p (k n) -> p k n", k=32)
            t_lw = p.dma("gpsimd", lambda e, j=j, wv=wv: e.dma_start(
                out=wv, in_=w3_in[j]), deps=h1_tb[s].wr(), ch=f"h1{s}")
            h1_tb[s].did_write(t_lw)
            ba, tba = pick(j % 4)
            mm = None
            for k in range(32):
                mm = p.op("tensor", lambda e, wv=wv, k=k, ba=ba:
                          e.matmul(ba, wv[:, k, :], hA[:, k * T:(k + 1) * T],
                                   start=(k == 0), stop=(k == 31)),
                          deps=((t_lw, hA_tb.w) + tba.wr()) if k == 0 else ())
            tba.did_write(mm)
            h1_tb[s].did_read(mm)
            hA_tb.did_read(mm)
            tu = p.op("vector", lambda e, ba=ba, j=j, cols=cols:
                      e.scalar_tensor_tensor(
                          out=ln_u(ctx, j), in0=ba, scalar=col("w3s", j),
                          in1=xres_f(ctx, j)[:, cols], op0=ALU.mult, op1=ALU.add),
                      deps=(mm,) + sc_tok + u_tb.wr() + xres_tb.rd())
            tba.did_read(tu)
            xres_tb.did_read(tu)
            u_tb.did_write(tu)
            u_toks.append(tu)
        if "ln" in phases and "w3" in phases:
            ln_chunk(ctx, tc, u_toks)
        elif "w3" not in phases:
            for j in range(8):
                t = p.op("vector", lambda e, j=j: e.memset(ln_u(ctx, j), 0.0),
                         deps=u_tb.wr())
                u_tb.did_write(t)
                u_toks.append(t)
            if "ln" in phases:
                ln_chunk(ctx, tc, u_toks)



def build_l1attn(ctx):
    nc, p = ctx.nc, ctx.p
    T = 512
    wq_in = nc.dram_tensor("wqkv", [8, 128, 8, 384], F32, kind="ExternalInput").ap()
    bq_in = nc.dram_tensor("bqkv", [128, 8, 3], F32, kind="ExternalInput").ap()
    wo_in = nc.dram_tensor("wout", [8, 128, DM], F32, kind="ExternalInput").ap()
    bo_in = nc.dram_tensor("bout", [128, 8], F32, kind="ExternalInput").ap()

    wqb = ctx.sba("at_wq", [128, 8 * 384], BF16)
    pool = ctx.sba("at_pool", [128, 4 * SEQ], BF16)
    bqc = ctx.sba("at_bq", [128, 8 * 3], F32)
    boc = ctx.sba("at_bo", [128, 8], F32)
    sclq = ctx.sba("at_sclq", [128, 3], F32)
    stage = ctx.sba("at_stage", [128, 3 * T], BF16)
    kscr = pool[:, 0:2 * SEQ]
    qscr = pool[:, 2 * SEQ:4 * SEQ]
    wob = pool[:, 0:8 * DM]
    vscr = ctx.sba("at_vscr", [128, 2 * NCH * 65], BF16)
    asb = ctx.sba("at_asb", [128, SEQ], BF16)
    rrow = ctx.sba("at_rrow", [128, 256], F32)
    rbc = ctx.sba("at_rbc", [128, 128], F32)
    aout = ctx.sba("at_out", [128, 8 * SEQ], BF16)
    rsc_b = asb[:, 0:T]
    rsc_f = asb[64:65, 1024:2048].bitcast(F32)

    xres_tb, xresb_tb = ctx.tb("x_res"), ctx.tb("x_resb")
    u_tb = ctx.tb("ln_u")
    pmA_tb = [ctx.tb(f"pmA{i}") for i in range(4)]
    pmB_tb = ctx.tb("pmB")
    ptr_tb = [ctx.tb("p_tr0"), ctx.tb("p_tr1")]
    wq_tb, wo_tb = ctx.tb("at_wq"), ctx.tb("at_wo")
    st_tb = ctx.tb("at_stage")
    ks_tb = [ctx.tb("at_ks0"), ctx.tb("at_ks1")]
    qs_tb = [ctx.tb("at_qs0"), ctx.tb("at_qs1")]
    vs_tb = [ctx.tb("at_vs0"), ctx.tb("at_vs1")]
    asb_tb = ctx.tb("at_asb")
    rr_tb, rbc_tb = ctx.tb("at_rrow"), ctx.tb("at_rbc")
    ao_tb = ctx.tb("at_out")
    rsc_tb = ctx.tb("at_asb")
    cdep = ctx.consts_tok

    tz1 = p.op("vector", lambda e: e.memset(kscr[64:128, :], 0.0),
               deps=ks_tb[0].wr() + ks_tb[1].wr())
    tz2 = p.op("vector", lambda e: e.memset(kscr[64:65, :], 1.0), deps=(tz1,))
    for tb in ks_tb:
        tb.did_write(tz2)
    tz3 = p.op("vector", lambda e: e.memset(qscr[64:128, :], 0.0),
               deps=qs_tb[0].wr() + qs_tb[1].wr())
    for tb in qs_tb:
        tb.did_write(tz3)
    tz4 = p.op("vector", lambda e: e.memset(
        vscr[:].rearrange("p (g c) -> p g c", c=65)[:, :, 64:65], 1.0),
        deps=vs_tb[0].wr() + vs_tb[1].wr())
    for tb in vs_tb:
        tb.did_write(tz4)
    tz5 = p.op("vector", lambda e: e.memset(rrow[:], 0.0), deps=ctx.tb("at_rrow").wr())
    ctx.tb("at_rrow").did_write(tz5)
    ts1 = p.op("vector", lambda e: e.memset(sclq[:, :], 1.0))
    ts2 = p.op("vector", lambda e: e.memset(sclq[0:64, 0:1], 0.125), deps=(ts1,))
    ts3 = p.op("vector", lambda e: e.memset(sclq[64:128, 1:2], 0.125), deps=(ts2,))
    scl_tok = (ts3,)
    t_bo = p.dma("sync", lambda e: e.dma_start(out=boc[:], in_=bo_in))
    t_bq0 = p.dma("sync", lambda e: e.dma_start(
        out=bqc[:].rearrange("p (a b) -> p a b", a=8), in_=bq_in))
    bq_f = []
    for i in range(3):
        bq_f.append(p.op("vector", lambda e, i=i: e.tensor_scalar(
            out=bqc[:].rearrange("p (a b) -> p a b", a=8)[:, :, i],
            in0=bqc[:].rearrange("p (a b) -> p a b", a=8)[:, :, i],
            scalar1=sclq[:, i:i + 1], scalar2=None, op0=ALU.mult),
            deps=(t_bq0,) + scl_tok))
    t_bq = bq_f[-1]


    for pr in range(8):
        hA, hB = 2 * pr, 2 * pr + 1
        t_lw = p.dma("gpsimd", lambda e, pr=pr: e.dma_start(
            out=wqb[:].rearrange("p (k n) -> p k n", k=8), in_=wq_in[pr]),
            deps=wq_tb.wr(), ch="at_wq")
        wq_tb.did_write(t_lw)
        t_km = p.op("vector", lambda e: e.memset(rrow[64:66, 0:2], -1e30),
                    deps=rr_tb.wr())
        rr_tb.did_write(t_km)
        for tc4 in range(4):
            cols = slice(tc4 * T, (tc4 + 1) * T)
            mm3 = []
            for i in range(3):
                mm = None
                for k in range(8):
                    mm = p.op("tensor", lambda e, i=i, k=k, cols=cols:
                              e.matmul(ctx.pmA[i][:, :],
                                       wqb[:, k * 384 + i * 128:k * 384 + (i + 1) * 128],
                                       ctx.x_resb[:, :].rearrange(
                                           "p (j t) -> p j t", j=8)[:, k, cols],
                                       start=(k == 0), stop=(k == 7)),
                              deps=((t_lw, xresb_tb.w) + pmA_tb[i].wr())
                              if k == 0 else ())
                pmA_tb[i].did_write(mm)
                mm3.append(mm)
            wq_tb.did_read(mm3[-1])
            xresb_tb.did_read(mm3[-1])
            ev = []
            for i in range(3):
                t = p.op("scalar", lambda e, i=i, pr=pr: e.activation(
                    out=stage[:, i * T:(i + 1) * T], in_=ctx.pmA[i][:, :],
                    func=AF.Identity, bias=bqc[:, pr * 3 + i:pr * 3 + i + 1],
                    scale=sclq[:, i:i + 1]),
                    deps=(mm3[i], t_bq) + scl_tok + st_tb.wr())
                pmA_tb[i].did_read(t)
                ev.append(t)
            st_tb.did_write(ev[-1])
            ev = tuple(ev)
            t1 = p.op("vector", lambda e, cols=cols: e.tensor_copy(
                out=qscr[0:64, 0 * SEQ:1 * SEQ][:, cols],
                in_=stage[0:64, 0:T]), deps=ev + qs_tb[0].wr())
            qs_tb[0].did_write(t1)
            t2 = p.op("vector", lambda e, cols=cols: e.tensor_copy(
                out=kscr[0:64, 1 * SEQ:2 * SEQ][:, cols],
                in_=stage[0:64, 2 * T:3 * T]), deps=ev + ks_tb[1].wr())
            ks_tb[1].did_write(t2)
            t3 = p.op("vector", lambda e, cols=cols: e.tensor_copy(
                out=kscr[0:64, 0 * SEQ:1 * SEQ][:, cols],
                in_=stage[64:128, 0:T]), deps=ev + ks_tb[0].wr())
            ks_tb[0].did_write(t3)
            t4 = p.op("vector", lambda e, cols=cols: e.tensor_copy(
                out=qscr[0:64, 1 * SEQ:2 * SEQ][:, cols],
                in_=stage[64:128, 1 * T:2 * T]), deps=ev + qs_tb[1].wr())
            qs_tb[1].did_write(t4)
            st_tb.did_read(t1)
            st_tb.did_read(t2)
            st_tb.did_read(t3)
            st_tb.did_read(t4)
            for kt in range(4):
                ktg = tc4 * 4 + kt
                for hh, (p0, coff) in enumerate(((0, T), (64, 2 * T))):
                    tv = p.op("tensor", lambda e, p0=p0, coff=coff, kt=kt:
                              e.transpose(
                                  ctx.p_tr[:, 0:512].bitcast(BF16)[:, 0:64],
                                  stage[p0:p0 + 64, coff + kt * 128:coff + (kt + 1) * 128],
                                  ctx.identb[p0:p0 + 64, p0:p0 + 64]),
                              deps=ev + ptr_tb[0].wr() + cdep)
                    ptr_tb[0].did_write(tv)
                    st_tb.did_read(tv)
                    tvc = p.op("vector", lambda e, hh=hh, ktg=ktg: e.tensor_copy(
                        out=vscr[:, (hh * NCH + ktg) * 65:(hh * NCH + ktg) * 65 + 64],
                        in_=ctx.p_tr[:, 0:512].bitcast(BF16)[:, 0:64]),
                        deps=(tv,) + vs_tb[hh].wr())
                    ptr_tb[0].did_read(tvc)
                    vs_tb[hh].did_write(tvc)
            for hh, qtok in ((0, t1), (1, t4)):
                tqs = p.op("scalar", lambda e, hh=hh, cols=cols: e.activation(
                    out=rsc_b[0:64, :],
                    in_=qscr[0:64, hh * SEQ:(hh + 1) * SEQ][:, cols],
                    func=AF.Square), deps=(qtok,) + rsc_tb.wr())
                rsc_tb.did_write(tqs)
                tqm = p.op("tensor", lambda e: e.matmul(
                    ctx.p_tr[64:65, 512:1024][:, 0:T], ctx.ones1[0:64, :],
                    rsc_b[0:64, :], start=True, stop=True),
                    deps=(tqs,) + ptr_tb[1].wr() + cdep)
                ptr_tb[1].did_write(tqm)
                rsc_tb.did_read(tqm)
                qb = (0 if hh == 0 else 64)
                tqc = p.op("vector", lambda e, hh=hh, cols=cols, pr=pr, qb=qb:
                           e.tensor_copy(
                    out=aout[qb:qb + 1, pr * SEQ:(pr + 1) * SEQ][:, cols],
                    in_=ctx.p_tr[64:65, 512:1024][:, 0:T]),
                    deps=(tqm,) + ao_tb.wr())
                ptr_tb[1].did_read(tqc)
                ao_tb.did_write(tqc)
            for hh, ktok in ((0, t3), (1, t2)):
                tks = p.op("scalar", lambda e, hh=hh, cols=cols: e.activation(
                    out=rsc_b[0:64, :],
                    in_=kscr[0:64, hh * SEQ:(hh + 1) * SEQ][:, cols],
                    func=AF.Square), deps=(ktok,) + rsc_tb.wr())
                rsc_tb.did_write(tks)
                tkm = p.op("tensor", lambda e: e.matmul(
                    ctx.p_tr[64:65, 512:1024][:, 0:T], ctx.ones1[0:64, :],
                    rsc_b[0:64, :], start=True, stop=True),
                    deps=(tks,) + ptr_tb[1].wr() + cdep)
                ptr_tb[1].did_write(tkm)
                rsc_tb.did_read(tkm)
                tkr = p.op("vector", lambda e: e.reduce_max(
                    out=rrow[64:65, 4:5], in_=ctx.p_tr[64:65, 512:1024][:, 0:T],
                    axis=AX.X), deps=(tkm,) + rr_tb.wr())
                ptr_tb[1].did_read(tkr)
                tkx = p.op("vector", lambda e, hh=hh: e.tensor_tensor(
                    out=rrow[64:65, hh:hh + 1], in0=rrow[64:65, hh:hh + 1],
                    in1=rrow[64:65, 4:5], op=ALU.max), deps=(tkr,),
                    sync_self=True)
                rr_tb.did_write(tkx)

        for hh in range(2):
            tlk = p.op("scalar", lambda e, hh=hh: e.activation(
                out=rrow[64:65, 6 + hh:7 + hh], in_=rrow[64:65, hh:hh + 1],
                func=AF.Ln), deps=rr_tb.rd() + rr_tb.wr())
            rr_tb.did_write(tlk)
            for tc4 in range(4):
                cols = slice(tc4 * T, (tc4 + 1) * T)
                qb = (0 if hh == 0 else 64)
                tl1 = p.op("scalar", lambda e, hh=hh, cols=cols, pr=pr, qb=qb:
                           e.activation(
                    out=rsc_f, in_=aout[qb:qb + 1, pr * SEQ:(pr + 1) * SEQ][:, cols],
                    func=AF.Ln), deps=(ao_tb.w, tlk) + rsc_tb.wr())
                ao_tb.did_read(tl1)
                rsc_tb.did_write(tl1)
                tl2 = p.op("vector", lambda e, hh=hh: e.tensor_scalar(
                    out=rsc_f, in0=rsc_f,
                    scalar1=rrow[64:65, 6 + hh:7 + hh], scalar2=0.5,
                    op0=ALU.add, op1=ALU.mult), deps=(tl1,) + rr_tb.rd())
                rr_tb.did_read(tl2)
                tl3 = p.op("scalar", lambda e: e.activation(
                    out=rsc_f, in_=rsc_f, func=AF.Exp),
                    deps=(tl2,), sync_self=True)
                tl4 = p.op("vector", lambda e, hh=hh, cols=cols: e.tensor_scalar(
                    out=qscr[64:65, hh * SEQ:(hh + 1) * SEQ][:, cols],
                    in0=rsc_f, scalar1=-1.0, scalar2=None,
                    op0=ALU.mult), deps=(tl3,))
                rsc_tb.did_write(tl4)
                rsc_tb.did_read(tl4)
                qs_tb[hh].did_write(tl4)

        for hh in range(2):
            h = hA + hh
            kv = kscr[:, hh * SEQ:(hh + 1) * SEQ]
            qv = qscr[:, hh * SEQ:(hh + 1) * SEQ]
            for qt in range(NCH):
                texp = []
                for bk in range(4):
                    mm = None
                    for kk in range(4):
                        kt = bk * 4 + kk
                        mm = p.op("tensor", lambda e, bk=bk, kk=kk, kt=kt, qt=qt,
                                  kv=kv, qv=qv:
                                  e.matmul(ctx.pmA[bk][:, kk * 128:(kk + 1) * 128],
                                           kv[:, kt * 128:(kt + 1) * 128],
                                           qv[:, qt * 128:(qt + 1) * 128],
                                           start=True, stop=True),
                                  deps=((ks_tb[hh].w, qs_tb[hh].w)
                                        + pmA_tb[bk].wr()) if kk == 0 else ())
                    pmA_tb[bk].did_write(mm)
                    ks_tb[hh].did_read(mm)
                    qs_tb[hh].did_read(mm)
                    te = p.op("scalar", lambda e, bk=bk: e.activation(
                        out=asb[:, bk * 512:(bk + 1) * 512], in_=ctx.pmA[bk][:, :],
                        func=AF.Exp), deps=(mm,) + (asb_tb.wr() if bk == 0 else ()))
                    pmA_tb[bk].did_read(te)
                    texp.append(te)
                asb_tb.did_write(texp[-1])
                mm = None
                for kt in range(NCH):
                    mm = p.op("tensor", lambda e, kt=kt, hh=hh: e.matmul(
                        ctx.pmB[0:65, 0:128],
                        vscr[:, (hh * NCH + kt) * 65:(hh * NCH + kt + 1) * 65],
                        asb[:, kt * 128:(kt + 1) * 128],
                        start=(kt == 0), stop=(kt == NCH - 1)),
                        deps=((texp[-1], vs_tb[hh].w) + pmB_tb.wr())
                        if kt == 0 else ())
                pmB_tb.did_write(mm)
                vs_tb[hh].did_read(mm)
                asb_tb.did_read(mm)
                tr_ = p.op("vector", lambda e: e.reciprocal(
                    out=rrow[64:65, 128:256], in_=ctx.pmB[64:65, 0:128]),
                    deps=(mm,) + rr_tb.wr())
                rr_tb.did_write(tr_)
                pmB_tb.did_read(tr_)
                tb_ = p.op("tensor", lambda e: e.matmul(
                    ctx.p_tr[0:64, 0:128], ctx.sel64[:, 0:64],
                    rrow[:, 128:256], start=True, stop=True),
                    deps=(tr_,) + ptr_tb[0].wr() + cdep)
                ptr_tb[0].did_write(tb_)
                rr_tb.did_read(tb_)
                tc_ = p.op("vector", lambda e: e.tensor_copy(
                    out=rbc[0:64, 0:128], in_=ctx.p_tr[0:64, 0:128]),
                    deps=(tb_,) + rbc_tb.wr())
                rbc_tb.did_write(tc_)
                ptr_tb[0].did_read(tc_)
                po = (h % 2) * 64
                to_ = p.op("vector", lambda e, h=h, qt=qt, po=po: e.tensor_tensor(
                    out=aout[po:po + 64,
                             (h // 2) * SEQ + qt * 128:(h // 2) * SEQ + (qt + 1) * 128],
                    in0=ctx.pmB[0:64, 0:128], in1=rbc[0:64, 0:128], op=ALU.mult),
                    deps=(tc_, mm) + (ao_tb.wr() if qt == 0 else ()))
                pmB_tb.did_read(to_)
                rbc_tb.did_read(to_)
                ao_tb.did_write(to_)

    t_wo = p.dma("gpsimd", lambda e: e.dma_start(
        out=wob.rearrange("p (k n) -> p k n", k=8),
        in_=wo_in.rearrange("k p n -> p k n")),
        deps=ks_tb[0].wr() + ks_tb[1].wr() + qs_tb[0].wr() + qs_tb[1].wr(),
        ch="at_wo")
    wo_tb.did_write(t_wo)
    for tc4 in range(4):
        cols = slice(tc4 * T, (tc4 + 1) * T)
        u_toks = []
        for j in range(8):
            bk = j % 4
            mm = None
            for k in range(8):
                mm = p.op("tensor", lambda e, j=j, k=k, bk=bk, cols=cols:
                          e.matmul(ctx.pmA[bk][:, :],
                                   wob[:, k * DM + j * 128:k * DM + (j + 1) * 128],
                                   aout[:, :].rearrange("p (a t) -> p a t", a=8)[:, k, cols],
                                   start=(k == 0), stop=(k == 7)),
                          deps=((t_wo, ao_tb.w) + pmA_tb[bk].wr()) if k == 0 else ())
                wo_tb.did_read(mm) if False else None
            pmA_tb[bk].did_write(mm)
            wo_tb.did_read(mm)
            ao_tb.did_read(mm)
            tu = p.op("vector", lambda e, j=j, bk=bk, cols=cols:
                      e.scalar_tensor_tensor(
                          out=ln_u(ctx, j), in0=ctx.pmA[bk][:, :],
                          scalar=boc[:, j:j + 1], in1=xres_f(ctx, j)[:, cols],
                          op0=ALU.add, op1=ALU.add),
                      deps=(mm, t_bo) + u_tb.wr() + xres_tb.rd())
            pmA_tb[bk].did_read(tu)
            xres_tb.did_read(tu)
            u_tb.did_write(tu)
            u_toks.append(tu)
        ln_chunk(ctx, tc4, u_toks)



def build_mamba(ctx):
    nc, p = ctx.nc, ctx.p
    ins = {}
    for d in ("f", "b"):
        for g2 in range(2):
            ins[f"wi_{d}{g2}"] = nc.dram_tensor(
                f"wi_{d}{g2}", [128, 8, 2320], F32, kind="ExternalInput").ap()
            ins[f"wo_{d}{g2}"] = nc.dram_tensor(
                f"wo_{d}{g2}", [128, 8, 1024], F32, kind="ExternalInput").ap()
        for nm, sh in (("convw", [128, 4, 18]), ("convb", [128, 18]),
                       ("alog", [16, 2]), ("dvec", [1, 32]), ("dtb", [16, 2])):
            ins[f"{nm}_{d}"] = nc.dram_tensor(
                f"{nm}_{d}", sh, F32, kind="ExternalInput").ap()
    ins["wm"] = nc.dram_tensor("wm", [128, 8, 1024], F32, kind="ExternalInput").ap()
    ins["bm"] = nc.dram_tensor("bm", [128, 8], F32, kind="ExternalInput").ap()
    ins["maskf"] = nc.dram_tensor("maskf", [128, 128], F32, kind="ExternalInput").ap()
    ins["maskb"] = nc.dram_tensor("maskb", [128, 128], F32, kind="ExternalInput").ap()

    import contextlib
    stack = contextlib.ExitStack()
    sb = lambda n, sh, dt: stack.enter_context(nc.sbuf_tensor(n, sh, dt))
    xm = ctx.sba("mb_xm", [128, 8 * SEQ], BF16)
    wib = ctx.sba("mb_wi", [128, 8 * 2320], BF16)
    wmb = wib[:, 0:8 * 1024]
    wob = sb("mb_wo", [128, 8 * 1024], BF16)
    cwb = sb("mb_cw", [128, 4 * 18 + 18 + 32], F32)
    dvr = sb("mb_dvr", [128, 32], F32)
    vec = sb("mb_vec", [16, 8], F32)
    msk = sb("mb_msk", [128, 2 * 128], F32)
    raw = sb("mb_raw", [128, 10 * (SCK + 3)], BF16)
    actS = sb("mb_act", [128, 10 * SCK], BF16)
    sz = sb("mb_sz", [128, 8 * SCK], BF16)
    halo = sb("mb_halo", [128, 30], BF16)
    xtk = sb("mb_xtk", [128, 8 * 128], BF16)
    btk = sb("mb_btk", [128, 128], BF16)
    cbm = sb("mb_cbm", [128, 128], BF16)
    dts = sb("mb_dts", [16, 3 * SCK + 128], F32)
    tok = sb("mb_tok", [128, 2 * 16], F32)
    hst = sb("mb_hst", [128, 16 * 64], BF16)
    Asc = sb("mb_A", [128, 2 * 128], F32)
    Dsc = sb("mb_D", [128, 4 * 128], F32)
    Esc = sb("mb_E", [128, 4 * 128], F32)
    Csc = sb("mb_C", [128, 4 * 128], BF16)
    Ssc = sb("mb_S", [128, 4 * 128], BF16)
    Xsc = sb("mb_X", [128, 8 * 64], BF16)
    ysc = sb("mb_y", [128, 128], F32)

    TB = ctx.tb
    xres_tb = TB("x_res")
    xm_tb = TB("mb_xm")
    wi_tb, wo_tb = TB("mb_wi"), TB("mb_wo")
    cw_tb, dv_tb, vec_tb, msk_tb = TB("mb_cw"), TB("mb_dvr"), TB("mb_vec"), TB("mb_msk")
    raw_tb, act_tb, halo_tb = TB("mb_raw"), TB("mb_act"), TB("mb_halo")
    sz_tb = TB("mb_sz")
    xtk_tb, btk_tb, cbm_tb = TB("mb_xtk"), TB("mb_btk"), TB("mb_cbm")
    dts_tb, tok_tb, hst_tb = TB("mb_dts"), TB("mb_tok"), TB("mb_hst")
    A_tb = [TB(f"mb_A{i}") for i in range(2)]
    D_tb = [TB(f"mb_D{i}") for i in range(4)]
    E_tb = [TB(f"mb_E{i}") for i in range(4)]
    C_tb = [TB(f"mb_C{i}") for i in range(4)]
    S_tb = [TB(f"mb_S{i}") for i in range(4)]
    X_tb = [TB(f"mb_X{i}") for i in range(4)]
    y_tb = [TB("mb_y0"), TB("mb_y1")]
    yg_tb = TB("mb_yg")
    pmA_tb = [TB(f"pmA{i}") for i in range(4)]
    pmB_tb = TB("pmB")
    pst_tb = TB("p_stat")
    ptr_tb = [TB("p_tr0"), TB("p_tr1")]
    cdep = ctx.consts_tok

    t_mf = p.dma("sync", lambda e: e.dma_start(out=msk[:, 0:128], in_=ins["maskf"]))
    t_mb = p.dma("sync", lambda e: e.dma_start(out=msk[:, 128:256], in_=ins["maskb"]))
    msk_tb.did_write(t_mb)
    msk_tok = (t_mf, t_mb)
    one_c, one_tok = ctx.cf(1.0)
    neg1_c, neg1_tok = ctx.cf(-1.0)
    half_c, half_tok = ctx.cf(0.5)
    zero_c, zero_tok = ctx.cf(0.0)
    p1_c, p1_tok = ctx.cf(0.1)
    cf_tok = (one_tok, neg1_tok, half_tok, zero_tok, p1_tok)
    ones16 = sb("mb_o16", [16, 128], F32)
    t_o = p.op("vector", lambda e: e.memset(ones16[:], 1.0))
    t_zd = p.op("vector", lambda e: e.memset(dvr[:], 0.0), deps=dv_tb.wr())
    init_tok = (t_o,)

    first_pass = [True]
    xtm_guard = TB("xtm0").wr() + TB("xtm1").wr()

    import os as _os
    _dirs = _os.environ.get("MB_DIRS", "fb")
    _grps = _os.environ.get("MB_GRPS", "01")
    for d in ("f", "b"):
        if d not in _dirs:
            continue
        fwd = (d == "f")
        mask01 = msk[:, 0:128] if fwd else msk[:, 128:256]
        last = 127 if fwd else 0
        t_dv = p.dma("sync", lambda e, d=d: e.dma_start(
            out=dvr[64:65, :], in_=ins[f"dvec_{d}"]), deps=(t_zd,) + dv_tb.wr())
        dv_tb.did_write(t_dv)
        t_db = p.op("tensor", lambda e: e.matmul(
            ctx.p_tr[:, 0:32], ctx.sel64[:, :], dvr[:, :],
            start=True, stop=True), deps=(t_dv,) + ptr_tb[0].wr() + cdep)
        ptr_tb[0].did_write(t_db)
        dv_tb.did_read(t_db)
        t_dc = p.op("vector", lambda e: e.tensor_copy(
            out=cwb[:, 90:122], in_=ctx.p_tr[:, 0:32]), deps=(t_db,) + cw_tb.wr())
        ptr_tb[0].did_read(t_dc)
        t_cw = p.dma("sync", lambda e, d=d: e.dma_start(
            out=cwb[:, 0:72].rearrange("p (a b) -> p a b", a=4),
            in_=ins[f"convw_{d}"]), deps=cw_tb.wr())
        t_cb = p.dma("sync", lambda e, d=d: e.dma_start(
            out=cwb[:, 72:90], in_=ins[f"convb_{d}"]), deps=cw_tb.wr())
        cw_tb.did_write(t_cb)
        cw_tok = (t_dc, t_cw, t_cb)

        for g2 in range(2):
            if str(g2) not in _grps:
                continue
            t_wi = p.dma("gpsimd", lambda e, d=d, g2=g2: e.dma_start(
                out=wib[:].rearrange("p (k n) -> p k n", k=8),
                in_=ins[f"wi_{d}{g2}"]), deps=wi_tb.wr(), ch="mb_wi")
            wi_tb.did_write(t_wi)
            t_wo = p.dma("gpsimd", lambda e, d=d, g2=g2: e.dma_start(
                out=wob[:].rearrange("p (k n) -> p k n", k=8),
                in_=ins[f"wo_{d}{g2}"]), deps=wo_tb.wr(), ch="mb_wo")
            wo_tb.did_write(t_wo)
            t_vc = p.dma("sync", lambda e, d=d, g2=g2: e.dma_start(
                out=vec[:, 0:1], in_=ins[f"dtb_{d}"][:, g2:g2 + 1],
                allow_slow_non_contiguous=True), deps=vec_tb.wr())
            vec_tb.did_write(t_vc)
            t_vc2 = p.dma("sync", lambda e, d=d, g2=g2: e.dma_start(
                out=vec[:, 1:2], in_=ins[f"alog_{d}"][:, g2:g2 + 1],
                allow_slow_non_contiguous=True), deps=(t_vc,))
            vec_tb.did_write(t_vc2)
            t_eA = p.op("scalar", lambda e: e.activation(
                out=vec[:, 2:3], in_=vec[:, 1:2], func=AF.Exp),
                deps=(t_vc2,))
            t_nA = p.op("vector", lambda e: e.tensor_scalar(
                out=vec[:, 2:3], in0=vec[:, 2:3], scalar1=-1.0, scalar2=None,
                op0=ALU.mult), deps=(t_eA,))
            vec_tb.did_write(t_nA)
            t_h0 = p.op("vector", lambda e: e.memset(hst[:], 0.0),
                        deps=hst_tb.wr())
            hst_tb.did_write(t_h0)
            t_r0 = p.op("vector", lambda e: e.memset(halo[:], 0.0),
                        deps=halo_tb.wr())
            halo_tb.did_write(t_r0)

            W3 = SCK + 3
            scs = range(SEQ // SCK) if fwd else range(SEQ // SCK - 1, -1, -1)
            for sci, sc_i in enumerate(scs):
                b0 = sc_i * SCK
                xbt = []
                for k in range(8):
                    t = p.op("vector", lambda e, k=k, b0=b0: e.tensor_copy(
                        out=actS[:, k * SCK:(k + 1) * SCK],
                        in_=xres_f(ctx, k)[:, b0:b0 + SCK]),
                        deps=xres_tb.rd() + (act_tb.wr() if k == 0 else ()))
                    xres_tb.did_read(t)
                    xbt.append(t)
                act_tb.did_write(xbt[-1])
                xb_dep = xbt[-1]

                def mmproj(out_ap, otile, ncols):
                    mm = None
                    for k in range(8):
                        def f(e, out_ap=out_ap, otile=otile, ncols=ncols, k=k):
                            return e.matmul(
                                out_ap,
                                wib[:, k * 2320 + otile * 128:
                                    k * 2320 + otile * 128 + ncols],
                                actS[:, k * SCK:(k + 1) * SCK],
                                start=(k == 0), stop=(k == 7))
                        dd = ()
                        if k == 0:
                            dd = (t_wi, xb_dep)
                        mm = p.op("tensor", f, deps=dd)
                    return mm

                sz_toks = []
                for j in range(8):
                    bank = ctx.pmA[j % 2]
                    btb = pmA_tb[j % 2]
                    p.wait("tensor", btb.wr())
                    mm = mmproj(bank[:, 0:SCK], j, 128)
                    btb.did_write(mm)
                    act_tb.did_read(mm)
                    tsz = p.op("scalar", lambda e, j=j, bank=bank: e.activation(
                        out=sz[:, j * SCK:(j + 1) * SCK], in_=bank[:, 0:SCK],
                        func=AF.Silu),
                        deps=(mm,) + (sz_tb.wr() if j == 0 else ()))
                    btb.did_read(tsz)
                    sz_toks.append(tsz)
                sz_tb.did_write(sz_toks[-1])

                raw_toks = []
                for i in range(10):
                    bank = ctx.pmA[2 + i % 2]
                    btb = pmA_tb[2 + i % 2]
                    p.wait("tensor", btb.wr())
                    mm = mmproj(bank[:, 0:SCK], 8 + i, 128)
                    btb.did_write(mm)
                    act_tb.did_read(mm)
                    if fwd:
                        t_hl = p.op("vector", lambda e, i=i: e.tensor_copy(
                            out=raw[:, i * W3:i * W3 + 3],
                            in_=halo[:, i * 3:(i + 1) * 3]),
                            deps=halo_tb.rd()
                            + (raw_tb.wr() if i == 0 else ()))
                        halo_tb.did_read(t_hl)
                        t_ev = p.op("vector", lambda e, i=i, bank=bank:
                                    e.tensor_copy(
                                        out=raw[:, i * W3 + 3:i * W3 + 3 + SCK],
                                        in_=bank[:, 0:SCK]),
                                    deps=(mm, t_hl))
                        t_hs = p.op("vector", lambda e, i=i: e.tensor_copy(
                            out=halo[:, i * 3:(i + 1) * 3],
                            in_=raw[:, i * W3 + SCK:i * W3 + SCK + 3]),
                            deps=(t_ev,) + halo_tb.wr())
                    else:
                        t_hl = p.op("vector", lambda e, i=i: e.tensor_copy(
                            out=raw[:, i * W3 + SCK:i * W3 + SCK + 3],
                            in_=halo[:, i * 3:(i + 1) * 3]),
                            deps=halo_tb.rd()
                            + (raw_tb.wr() if i == 0 else ()))
                        halo_tb.did_read(t_hl)
                        t_ev = p.op("vector", lambda e, i=i, bank=bank:
                                    e.tensor_copy(
                                        out=raw[:, i * W3:i * W3 + SCK],
                                        in_=bank[:, 0:SCK]),
                                    deps=(mm, t_hl))
                        t_hs = p.op("vector", lambda e, i=i: e.tensor_copy(
                            out=halo[:, i * 3:(i + 1) * 3],
                            in_=raw[:, i * W3:i * W3 + 3]),
                            deps=(t_ev,) + halo_tb.wr())
                    halo_tb.did_write(t_hs)
                    btb.did_read(t_ev)
                    raw_toks.append(t_ev)
                raw_tb.did_write(raw_toks[-1])
                p.wait("tensor", pmB_tb.wr())
                t_dtp = mmproj(ctx.pmB[0:16, 0:SCK], 18, 16)
                pmB_tb.did_write(t_dtp)
                wi_tb.did_read(t_dtp)
                act_tb.did_read(t_dtp)
                t_dte = p.op("vector", lambda e: e.tensor_copy(
                    out=dts[:, 0:SCK], in_=ctx.pmB[0:16, 0:SCK]),
                    deps=(t_dtp,) + dts_tb.wr())
                pmB_tb.did_read(t_dte)
                dts_tb.did_write(t_dte)

                t_u = p.op("vector", lambda e: e.tensor_scalar(
                    out=dts[:, 0:SCK], in0=dts[:, 0:SCK],
                    scalar1=vec[0:16, 0:1], scalar2=None, op0=ALU.add),
                    deps=(t_dte, t_nA), sync_self=True)
                vec_tb.did_read(t_u)
                t_au = p.op("scalar", lambda e: e.activation(
                    out=dts[:, SCK:2 * SCK], in_=dts[:, 0:SCK], func=AF.Abs),
                    deps=(t_u,))
                t_e1 = p.op("scalar", lambda e: e.activation(
                    out=dts[:, SCK:2 * SCK], in_=dts[:, SCK:2 * SCK],
                    func=AF.Exp, scale=neg1_c[0:16, :]),
                    deps=(t_au, neg1_tok), sync_self=True)
                t_l1 = p.op("scalar", lambda e: e.activation(
                    out=dts[:, SCK:2 * SCK], in_=dts[:, SCK:2 * SCK],
                    func=AF.Ln, bias=one_c[0:16, :]),
                    deps=(t_e1, one_tok), sync_self=True)
                t_r1 = p.op("vector", lambda e: e.tensor_scalar(
                    out=dts[:, 2 * SCK:3 * SCK], in0=dts[:, 0:SCK],
                    scalar1=0.0, scalar2=None, op0=ALU.max), deps=(t_u,))

                cacc_tb = [TB(f"mb_cacc{s}") for s in range(2)]
                cv_toks = []
                shifts = (0, 1, 2, 3) if fwd else (3, 2, 1, 0)
                for i in range(10):
                    s = i % 2
                    acc = ctx.xtm[:, s * 512:s * 512 + SCK]
                    gi = (g2 * 8 + i) if i < 8 else (16 + i - 8)
                    t1 = p.op("vector", lambda e, i=i, gi=gi, s0=shifts[0],
                              acc=acc: e.tensor_scalar(
                                  out=acc,
                                  in0=raw[:, i * W3 + s0:i * W3 + s0 + SCK],
                                  scalar1=cwb[:, gi:gi + 1],
                                  scalar2=cwb[:, 72 + gi:73 + gi],
                                  op0=ALU.mult, op1=ALU.add),
                              deps=(raw_toks[i],) + cw_tok + cacc_tb[s].wr()
                              + xtm_guard)
                    tlast = t1
                    for j in range(1, 4):
                        tlast = p.op("vector", lambda e, i=i, gi=gi, j=j,
                                     sj=shifts[j], acc=acc:
                                     e.scalar_tensor_tensor(
                                         out=acc,
                                         in0=raw[:, i * W3 + sj:i * W3 + sj + SCK],
                                         scalar=cwb[:, j * 18 + gi:j * 18 + gi + 1],
                                         in1=acc, op0=ALU.mult, op1=ALU.add),
                                     deps=(tlast,))
                    cacc_tb[s].did_write(tlast)
                    ts = p.op("scalar", lambda e, i=i, acc=acc: e.activation(
                        out=actS[:, i * SCK:(i + 1) * SCK], in_=acc,
                        func=AF.Silu),
                        deps=(tlast,) + (act_tb.wr() if i == 0 else ()))
                    cacc_tb[s].did_read(ts)
                    cv_toks.append(ts)
                act_tb.did_write(cv_toks[-1])
                cw_tb.did_read(cv_toks[-1])
                raw_tb.did_read(cv_toks[-1])

                t_sp = p.op("vector", lambda e: e.tensor_tensor(
                    out=dts[:, 2 * SCK:3 * SCK], in0=dts[:, 2 * SCK:3 * SCK],
                    in1=dts[:, SCK:2 * SCK], op=ALU.add), deps=(t_l1, t_r1),
                    sync_self=True)
                t_lg = p.op("vector", lambda e: e.tensor_scalar(
                    out=dts[:, SCK:2 * SCK], in0=dts[:, 2 * SCK:3 * SCK],
                    scalar1=vec[0:16, 2:3], scalar2=None, op0=ALU.mult),
                    deps=(t_sp, t_nA))
                vec_tb.did_read(t_lg)
                dts_tb.did_write(t_lg)

                Lc = 3 * SCK
                for qq in range(CPS):
                    q = qq if fwd else CPS - 1 - qq
                    c0 = b0 + q * Q
                    qo = q * Q
                    t_L = p.op("vector", lambda e, qo=qo: e.tensor_tensor_scan(
                        out=dts[:, Lc:Lc + 128], data0=ones16[:],
                        data1=dts[:, SCK + qo:SCK + qo + 128],
                        initial=0.0, op0=ALU.mult, op1=ALU.add),
                        deps=(t_lg,) + init_tok + dts_tb.wr(), sync_self=True)
                    if not fwd:
                        t_L = p.op("vector", lambda e: e.tensor_copy(
                            out=vec[:, 3:4], in_=dts[:, Lc + 127:Lc + 128]),
                            deps=(t_L,), sync_self=True)
                        t_L = p.op("vector", lambda e: e.tensor_scalar(
                            out=dts[:, Lc:Lc + 128], in0=dts[:, Lc:Lc + 128],
                            scalar1=vec[0:16, 3:4], scalar2=-1.0,
                            op0=ALU.subtract, op1=ALU.mult), deps=(t_L,),
                            sync_self=True)
                        t_L = p.op("vector", lambda e, qo=qo: e.tensor_tensor(
                            out=dts[:, Lc:Lc + 128], in0=dts[:, Lc:Lc + 128],
                            in1=dts[:, SCK + qo:SCK + qo + 128], op=ALU.add),
                            deps=(t_L,), sync_self=True)
                    dts_tb.did_write(t_L)
                    t_Lt = p.op("tensor", lambda e: e.transpose(
                        ctx.p_tr[:, 512:528], dts[0:16, Lc:Lc + 128],
                        ctx.identf[0:16, 0:16]),
                        deps=(t_L,) + ptr_tb[1].wr() + cdep)
                    ptr_tb[1].did_write(t_Lt)
                    dts_tb.did_read(t_Lt)
                    t_nL = p.op("scalar", lambda e: e.activation(
                        out=tok[:, 0:16], in_=ctx.p_tr[:, 512:528],
                        func=AF.Identity, scale=neg1_c),
                        deps=(t_Lt, neg1_tok), sync_self=True)
                    ptr_tb[1].did_read(t_nL)
                    tok_tb.did_write(t_nL)
                    t_dt2 = p.op("tensor", lambda e, qo=qo: e.transpose(
                        ctx.p_tr[:, 640:656],
                        dts[0:16, 2 * SCK + qo:2 * SCK + qo + 128],
                        ctx.identf[0:16, 0:16]),
                        deps=(t_sp, t_nL) + ptr_tb[1].wr() + cdep)
                    ptr_tb[1].did_write(t_dt2)
                    dts_tb.did_read(t_dt2)
                    t_dt3 = p.op("vector", lambda e: e.tensor_copy(
                        out=tok[:, 16:32], in_=ctx.p_tr[:, 640:656]),
                        deps=(t_dt2,))
                    ptr_tb[1].did_read(t_dt3)
                    tok_tb.did_write(t_dt3)

                    xt_toks = []
                    for i in range(8):
                        sl = i % 2
                        pv = ctx.p_tr[:, sl * 512:sl * 512 + 256].bitcast(BF16)[:, 0:128]
                        t_tr = p.op("tensor", lambda e, i=i, pv=pv, qo=qo:
                                    e.transpose(
                                        pv,
                                        actS[:, i * SCK + qo:i * SCK + qo + 128],
                                        ctx.identb[:]),
                                    deps=(cv_toks[i],) + ptr_tb[sl].wr() + cdep)
                        ptr_tb[sl].did_write(t_tr)
                        act_tb.did_read(t_tr)
                        t_cp = p.op("vector", lambda e, i=i, pv=pv: e.tensor_copy(
                            out=xtk[:, i * 128:(i + 1) * 128], in_=pv),
                            deps=(t_tr,) + (xtk_tb.wr() if i == 0 else ()))
                        ptr_tb[sl].did_read(t_cp)
                        xt_toks.append(t_cp)
                    xtk_tb.did_write(xt_toks[-1])
                    pvb = ctx.p_tr[:, 0:256].bitcast(BF16)[:, 0:128]
                    t_bt = p.op("tensor", lambda e, pvb=pvb, qo=qo: e.transpose(
                        pvb, actS[:, 8 * SCK + qo:8 * SCK + qo + 128],
                        ctx.identb[:]),
                        deps=(cv_toks[8],) + ptr_tb[0].wr() + cdep)
                    ptr_tb[0].did_write(t_bt)
                    act_tb.did_read(t_bt)
                    t_btc = p.op("vector", lambda e, pvb=pvb: e.tensor_copy(
                        out=btk[:], in_=pvb), deps=(t_bt,) + btk_tb.wr())
                    ptr_tb[0].did_read(t_btc)
                    btk_tb.did_write(t_btc)

                    p.wait("tensor", pmA_tb[0].wr())
                    t_cbm = p.op("tensor", lambda e, qo=qo: e.matmul(
                        ctx.pmA[0][:, 0:128],
                        actS[:, 8 * SCK + qo:8 * SCK + qo + 128],
                        actS[:, 9 * SCK + qo:9 * SCK + qo + 128],
                        start=True, stop=True),
                        deps=(cv_toks[8], cv_toks[9]))
                    pmA_tb[0].did_write(t_cbm)
                    act_tb.did_read(t_cbm)
                    t_cbe = p.op("vector", lambda e, mask01=mask01: e.tensor_tensor(
                        out=cbm[:], in0=ctx.pmA[0][:, 0:128], in1=mask01,
                        op=ALU.mult), deps=(t_cbm,) + msk_tok + cbm_tb.wr())
                    pmA_tb[0].did_read(t_cbe)
                    cbm_tb.did_write(t_cbe)

                    PV, TA, TBC, TEt, TDP, TMN, TD3h = {}, {}, {}, {}, {}, {}, {}
                    TXt, TX2, TSt, TCt = {}, {}, {}, {}
                    M2h, MSh, TY2 = {}, {}, {}

                    def hs_s0(hh):
                        sl2 = hh % 2
                        Av = Asc[:, sl2 * 128:(sl2 + 1) * 128]
                        t = p.op("scalar", lambda e, hh=hh, Av=Av: e.activation(
                            out=Av, in_=ctx.identf[:], func=AF.Identity,
                            scale=zero_c, bias=tok[:, hh:hh + 1]),
                            deps=(t_nL, zero_tok) + A_tb[sl2].wr() + cdep)
                        A_tb[sl2].did_write(t)
                        tok_tb.did_read(t)
                        TA[hh] = t

                    def hs_p1(hh):
                        sl2 = hh % 2
                        Av = Asc[:, sl2 * 128:(sl2 + 1) * 128]
                        pv = ctx.p_tr[:, sl2 * 512:sl2 * 512 + 128]
                        PV[hh] = pv
                        t = p.op("tensor", lambda e, Av=Av, pv=pv: e.transpose(
                            pv, Av, ctx.identf[:]),
                            deps=(TA[hh],) + ptr_tb[sl2].wr() + cdep)
                        ptr_tb[sl2].did_write(t)
                        A_tb[sl2].did_read(t)
                        TBC[hh] = t

                    def hs_s1(hh):
                        sl = hh % 4
                        pv = PV[hh]
                        Ev = Esc[:, sl * 128:(sl + 1) * 128]
                        Dv = Dsc[:, sl * 128:(sl + 1) * 128]
                        tE = p.op("scalar", lambda e, Ev=Ev, pv=pv: e.activation(
                            out=Ev, in_=pv, func=AF.Exp, scale=neg1_c),
                            deps=(TBC[hh], neg1_tok) + E_tb[sl].wr())
                        E_tb[sl].did_write(tE)
                        ptr_tb[hh % 2].did_read(tE)
                        TEt[hh] = tE
                        tDp = p.op("scalar", lambda e, Dv=Dv, pv=pv, hh=hh:
                                   e.activation(out=Dv, in_=pv,
                                                func=AF.Identity,
                                                scale=neg1_c,
                                                bias=tok[:, hh:hh + 1]),
                                   deps=(TBC[hh], t_nL, neg1_tok) + D_tb[sl].wr())
                        D_tb[sl].did_write(tDp)
                        ptr_tb[hh % 2].did_read(tDp)
                        tok_tb.did_read(tDp)
                        TDP[hh] = tDp

                    def hs_v1(hh):
                        sl = hh % 4
                        Dv = Dsc[:, sl * 128:(sl + 1) * 128]
                        t = p.op("vector", lambda e, Dv=Dv: e.tensor_scalar(
                            out=Dv, in0=Dv, scalar1=0.0, scalar2=None,
                            op0=ALU.min), deps=(TDP[hh],))
                        D_tb[sl].did_write(t)
                        TMN[hh] = t

                    def hs_s2(hh):
                        sl = hh % 4
                        Dv = Dsc[:, sl * 128:(sl + 1) * 128]
                        t3 = p.op("scalar", lambda e, Dv=Dv: e.activation(
                            out=Dv, in_=Dv, func=AF.Exp), deps=(TMN[hh],))
                        D_tb[sl].did_write(t3)
                        TD3h[hh] = t3
                        xti = hh // 2
                        xoff = xti * 128 + (hh % 2) * 64
                        Xv = Xsc[:, sl * 64:(sl + 1) * 64]
                        X2v = Xsc[:, 256 + sl * 64:256 + (sl + 1) * 64]
                        tX = p.op("scalar", lambda e, Xv=Xv, xoff=xoff, hh=hh:
                                  e.activation(out=Xv,
                                               in_=xtk[:, xoff:xoff + 64],
                                               func=AF.Identity,
                                               scale=tok[:, 16 + hh:17 + hh]),
                                  deps=(xt_toks[xti], t_dt3) + X_tb[sl].wr())
                        tok_tb.did_read(tX)
                        xtk_tb.did_read(tX)
                        tX2 = p.op("scalar", lambda e, Xv=Xv, X2v=X2v, Dv=Dv,
                                   last=last: e.activation(
                                       out=X2v, in_=Xv, func=AF.Identity,
                                       scale=Dv[:, last:last + 1]),
                                   deps=(tX, t3))
                        X_tb[sl].did_write(tX2)
                        D_tb[sl].did_read(tX2)
                        TXt[hh] = tX
                        TX2[hh] = tX2

                    def hs_v2(hh):
                        sl = hh % 4
                        Dv = Dsc[:, sl * 128:(sl + 1) * 128]
                        Ev = Esc[:, sl * 128:(sl + 1) * 128]
                        Sv = Ssc[:, sl * 128:(sl + 1) * 128]
                        Cv = Csc[:, sl * 128:(sl + 1) * 128]
                        tS = p.op("vector", lambda e, Sv=Sv, Dv=Dv: e.tensor_tensor(
                            out=Sv, in0=cbm[:], in1=Dv, op=ALU.mult),
                            deps=(TD3h[hh], t_cbe) + S_tb[sl].wr())
                        S_tb[sl].did_write(tS)
                        cbm_tb.did_read(tS)
                        D_tb[sl].did_read(tS)
                        tC = p.op("vector", lambda e, Cv=Cv, Ev=Ev, qo=qo:
                                  e.tensor_tensor(
                                      out=Cv,
                                      in0=actS[:, 9 * SCK + qo:9 * SCK + qo + 128],
                                      in1=Ev, op=ALU.mult),
                                  deps=(TEt[hh], cv_toks[9]) + C_tb[sl].wr())
                        C_tb[sl].did_write(tC)
                        E_tb[sl].did_read(tC)
                        act_tb.did_read(tC)
                        TSt[hh] = tS
                        TCt[hh] = tC

                    def hs_p2(hh):
                        sl = hh % 4
                        Sv = Ssc[:, sl * 128:(sl + 1) * 128]
                        Cv = Csc[:, sl * 128:(sl + 1) * 128]
                        Xv = Xsc[:, sl * 64:(sl + 1) * 64]
                        X2v = Xsc[:, 256 + sl * 64:256 + (sl + 1) * 64]
                        ybank = (ctx.pmA[1], ctx.pmA[2], ctx.pmA[3], ctx.pmB)[hh // 4]
                        ytb = (pmA_tb[1], pmA_tb[2], pmA_tb[3], pmB_tb)[hh // 4]
                        ypo = (hh % 2) * 64
                        ycol = ((hh % 4) // 2) * 128
                        yv = ybank[ypo:ypo + 64, ycol:ycol + 128]
                        m1 = p.op("tensor", lambda e, yv=yv, Xv=Xv, Sv=Sv:
                                  e.matmul(yv, Xv, Sv, start=True, stop=False),
                                  deps=(TXt[hh], TSt[hh]) + ytb.wr())
                        m2 = p.op("tensor", lambda e, yv=yv, Cv=Cv, hh=hh:
                                  e.matmul(yv, hst[:, hh * 64:(hh + 1) * 64],
                                           Cv, start=False, stop=True),
                                  deps=(TCt[hh], hst_tb.w if hst_tb.w else None))
                        ytb.did_write(m2)
                        S_tb[sl].did_read(m2)
                        C_tb[sl].did_read(m2)
                        hst_tb.did_read(m2)
                        X_tb[sl].did_read(m2)
                        soff = 128 + (hh % 3) * 128
                        sv = ctx.pmA[0][:, soff:soff + 64]
                        ms = p.op("tensor", lambda e, sv=sv, X2v=X2v: e.matmul(
                            sv, btk[:], X2v, start=True, stop=True),
                            deps=(TX2[hh], t_btc) + pmA_tb[0].wr())
                        pmA_tb[0].did_write(ms)
                        btk_tb.did_read(ms)
                        X_tb[sl].did_read(ms)
                        M2h[hh] = m2
                        MSh[hh] = ms

                    def hs_v3(hh):
                        sl = hh % 4
                        Ev = Esc[:, sl * 128:(sl + 1) * 128]
                        ybank = (ctx.pmA[1], ctx.pmA[2], ctx.pmA[3], ctx.pmB)[hh // 4]
                        ytb = (pmA_tb[1], pmA_tb[2], pmA_tb[3], pmB_tb)[hh // 4]
                        ypo = (hh % 2) * 64
                        ycol = ((hh % 4) // 2) * 128
                        yv = ybank[ypo:ypo + 64, ycol:ycol + 128]
                        soff = 128 + (hh % 3) * 128
                        sv = ctx.pmA[0][:, soff:soff + 64]
                        th = p.op("vector", lambda e, sv=sv, hh=hh, Ev=Ev,
                                  last=last: e.scalar_tensor_tensor(
                                      out=hst[:, hh * 64:(hh + 1) * 64],
                                      in0=hst[:, hh * 64:(hh + 1) * 64],
                                      scalar=Ev[:, last:last + 1], in1=sv,
                                      op0=ALU.mult, op1=ALU.add),
                                  deps=(MSh[hh], M2h[hh], TEt[hh]),
                                  sync_self=True)
                        hst_tb.did_write(th)
                        pmA_tb[0].did_read(th)
                        E_tb[sl].did_read(th)
                        ysl = ysc[ypo:ypo + 64, 0:128]
                        gh = g2 * 16 + hh
                        tile = hh // 2
                        rows = (hh % 2) * 64
                        ty1 = p.op("vector", lambda e, ysl=ysl, gh=gh, yv=yv,
                                   tile=tile, rows=rows, qo=qo:
                                   e.scalar_tensor_tensor(
                                       out=ysl,
                                       in0=actS[rows:rows + 64,
                                                tile * SCK + qo:
                                                tile * SCK + qo + 128],
                                       scalar=cwb[rows:rows + 64,
                                                  90 + gh:91 + gh],
                                       in1=yv, op0=ALU.mult, op1=ALU.add),
                                   deps=(M2h[hh], t_dc) + y_tb[hh % 2].wr())
                        y_tb[hh % 2].did_write(ty1)
                        ytb.did_read(ty1)
                        act_tb.did_read(ty1)
                        cw_tb.did_read(ty1)
                        ty2 = p.op("vector", lambda e, ysl=ysl, tile=tile,
                                   rows=rows, qo=qo: e.tensor_tensor(
                                       out=raw[rows:rows + 64,
                                               tile * SCK + qo:
                                               tile * SCK + qo + 128],
                                       in0=ysl,
                                       in1=sz[rows:rows + 64,
                                              tile * SCK + qo:
                                              tile * SCK + qo + 128],
                                       op=ALU.mult),
                                   deps=(ty1, sz_toks[-1]))
                        y_tb[hh % 2].did_read(ty2)
                        sz_tb.did_read(ty2)
                        raw_tb.did_write(ty2)
                        TY2[hh] = ty2

                    for it in range(21):
                        if it < 16:
                            hs_s0(it)
                        if 0 <= it - 1 < 16:
                            hs_p1(it - 1)
                        if 0 <= it - 5 < 16:
                            hs_v3(it - 5)
                        if 0 <= it - 4 < 16:
                            hs_v2(it - 4)
                            hs_p2(it - 4)
                        if 0 <= it - 3 < 16:
                            hs_s2(it - 3)
                        if 0 <= it - 2 < 16:
                            hs_s1(it - 2)
                            hs_v1(it - 2)
                    yg_last = TY2[15]

                for j in range(8):
                    bank = ctx.pmA[j % 2]
                    btb = pmA_tb[j % 2]
                    p.wait("tensor", btb.wr())
                    mm = None
                    for k in range(8):
                        mm = p.op("tensor", lambda e, j=j, k=k, bank=bank:
                                  e.matmul(bank[:, 0:SCK],
                                           wob[:, k * 1024 + j * 128:
                                               k * 1024 + (j + 1) * 128],
                                           raw[:, k * SCK:(k + 1) * SCK],
                                           start=(k == 0), stop=(k == 7)),
                                  deps=(t_wo, yg_last) if k == 0 else ())
                    btb.did_write(mm)
                    wo_tb.did_read(mm)
                    raw_tb.did_read(mm)
                    if first_pass[0]:
                        te = p.op("vector", lambda e, j=j, b0=b0, bank=bank:
                                  e.tensor_copy(
                                      out=xm[:, j * SEQ + b0:j * SEQ + b0 + SCK],
                                      in_=bank[:, 0:SCK]),
                                  deps=(mm,) + xm_tb.wr())
                    else:
                        te = p.op("vector", lambda e, j=j, b0=b0, bank=bank:
                                  e.tensor_tensor(
                                      out=xm[:, j * SEQ + b0:j * SEQ + b0 + SCK],
                                      in0=bank[:, 0:SCK],
                                      in1=xm[:, j * SEQ + b0:j * SEQ + b0 + SCK],
                                      op=ALU.add),
                                  deps=(mm,) + xm_tb.wr())
                    btb.did_read(te)
                    xm_tb.did_write(te)
            first_pass[0] = False

    stack.close()
    p.barrier()
    alloc_ln(ctx)
    t_wm = p.dma("gpsimd", lambda e: e.dma_start(
        out=wmb.rearrange("p (k n) -> p k n", k=8), in_=ins["wm"]),
        deps=wi_tb.wr(), ch="mb_wi")
    wi_tb.did_write(t_wm)
    bmc = ctx.sba("mb_bm", [128, 8], F32)
    t_bm = p.dma("sync", lambda e: e.dma_start(out=bmc[:], in_=ins["bm"]))
    tnh = ctx.sba("mb_tnh", [128, 512], BF16)
    tnh_tb = TB("mb_tnh")
    u_tb = TB("ln_u")
    for tc4 in range(4):
        cols = slice(tc4 * 512, (tc4 + 1) * 512)
        u_toks = []
        for j in range(8):
            bank = ctx.pmA[j % 2]
            btb = pmA_tb[j % 2]
            mm = None
            for k in range(8):
                mm = p.op("tensor", lambda e, j=j, k=k, bank=bank, cols=cols:
                          e.matmul(
                    bank[:, :],
                    wmb[:, k * 1024 + j * 128:k * 1024 + (j + 1) * 128],
                    xm[:, k * SEQ:(k + 1) * SEQ][:, cols],
                    start=(k == 0), stop=(k == 7)),
                    deps=((t_wm, xm_tb.w) + btb.wr()) if k == 0 else ())
            xm_tb.did_read(mm)
            btb.did_write(mm)
            wi_tb.did_read(mm)
            tt = p.op("scalar", lambda e, j=j, bank=bank: e.activation(
                out=tnh[:, :], in_=bank[:, :], func=AF.Tanh,
                bias=bmc[:, j:j + 1]), deps=(mm, t_bm) + tnh_tb.wr())
            tnh_tb.did_write(tt)
            btb.did_read(tt)
            tu1 = p.op("vector", lambda e, j=j, cols=cols: e.scalar_tensor_tensor(
                out=ln_u(ctx, j), in0=tnh[:, :], scalar=0.1,
                in1=xm[:, j * SEQ:(j + 1) * SEQ][:, cols],
                op0=ALU.mult, op1=ALU.add),
                deps=(tt, xm_tb.w) + u_tb.wr())
            tnh_tb.did_read(tu1)
            xm_tb.did_read(tu1)
            tu2 = p.op("vector", lambda e, j=j, cols=cols: e.tensor_tensor(
                out=ln_u(ctx, j), in0=ln_u(ctx, j),
                in1=xres_f(ctx, j)[:, cols], op=ALU.add),
                deps=(tu1,) + xres_tb.rd())
            xres_tb.did_read(tu2)
            u_tb.did_write(tu2)
            u_toks.append(tu2)
        ln_chunk(ctx, tc4, u_toks)



def fixed_sbuf(nc, name, shape, dtype, addr):
    h = nc.virtual_sbuf_tensor(name, shape, dtype)
    nc._tensor(h.name, h.shape, h.dtype, type="SB")
    mloc = nc.lookup_mloc(h)
    mloc.allocated = True
    mloc.addr = addr
    return h


def make_xresb(ctx):
    p = ctx.p
    xres_tb, xresb_tb = ctx.tb("x_res"), ctx.tb("x_resb")
    toks = []
    for j in range(8):
        t = p.op("vector", lambda e, j=j: e.tensor_copy(
            out=xres_b(ctx, j), in_=xres_f(ctx, j)),
            deps=xres_tb.rd() + (xresb_tb.wr() if j == 0 else ()))
        xres_tb.did_read(t)
        toks.append(t)
    xresb_tb.did_write(toks[-1])


def build_fused_program():
    import contextlib
    nc = bass.Bass("TRN2", target_bir_lowering=False, debug=False)
    ctx = Ctx(nc)
    ctx.ln_fixed = True
    ctx.ln_need_memset = False
    alloc_consts(ctx)
    ctx.p_stat = nc.alloc_psum_tensor("p_stat", [128, 512], F32)
    ctx.p_bc = ctx.p_tr
    alloc_xres(ctx, with_bf16=False)
    arena = nc.auto_arena()
    ctx.x_resb = arena.sbuf_tensor("x_resb", [128, 8 * SEQ], BF16)
    ctx.ln_u = arena.sbuf_tensor("ln_u", [128, 8 * 512], F32)
    ctx.ln_ub = arena.sbuf_tensor("ln_ub", [128, 8 * 512], BF16)
    ctx.ln_rows = arena.sbuf_tensor("ln_rows", [128, 3 * 512], F32)
    ctx.ln_rowsb = arena.sbuf_tensor("ln_rowsb", [128, 2 * 512], BF16)
    ctx.ln_bc = arena.sbuf_tensor("ln_bc", [128, 2 * 512], F32)
    arena.allocate("right")
    arena.free()
    load_consts(ctx)
    load_x(ctx)
    for which in ("mamba", "l0ffn", "l1attn", "l1ffn"):
        stack = contextlib.ExitStack()
        ctx.sba = (lambda stack: lambda n, sh, dt:
                   stack.enter_context(nc.sbuf_tensor(n, sh, dt)))(stack)
        ctx.ln_need_memset = True
        if which == "mamba":
            build_mamba(ctx)
        elif which == "l0ffn":
            alloc_ln(ctx)
            build_l0ffn(ctx)
        elif which == "l1attn":
            alloc_ln(ctx)
            make_xresb(ctx)
            build_l1attn(ctx)
        else:
            alloc_ln(ctx)
            make_xresb(ctx)
            build_l1ffn(ctx)
        ctx.p.full_barrier()
        stack.close()
    store_x(ctx)
    ctx.p.finalize()
    return nc


def kernel(**inputs):
    staged = stage_inputs({k: np.ascontiguousarray(np.asarray(v))
                           for k, v in inputs.items()})
    x = np.ascontiguousarray(np.asarray(inputs["L0_attn_x"], np.float32))
    nc = build_fused_program()
    names = set(_block_input_names(nc))
    in_maps = []
    for c in range(NCORES):
        m = {"x_in": np.ascontiguousarray(x[c]),
             "c_ident": staged["ident"], "c_sel0": staged["sel0"],
             "c_sel64": staged["sel64"]}
        for n in names:
            if n in m or n == "partition_id":
                continue
            m[n] = staged[_stage_key(n)]
        in_maps.append(m)
    res = run_bass_kernel_spmd(nc, in_maps, core_ids=list(range(NCORES)))
    out = np.stack([np.asarray(r["x_out"]) for r in res.results])
    return np.ascontiguousarray(out.astype(np.float32))
